# version 06
# baseline (speedup 1.0000x reference)
import contextlib
import os
import numpy as np
import ml_dtypes
import concourse.bass as bass
import concourse.mybir as mybir
from concourse.bass_utils import run_bass_kernel_spmd

F32 = mybir.dt.float32
BF16 = mybir.dt.bfloat16
I32 = mybir.dt.int32
AF = mybir.ActivationFunctionType
ALU = mybir.AluOpType

B, L, E, H, D = 4, 4096, 2048, 16, 128
INTER = 5504
INTP = 5632
NIC = 43
Q = 2048
SCALE = 1.0 / float(np.sqrt(D))
SLW = 1376
NGU = 128



import contextlib as _ctxlib
from concourse.ordered_engine_set import OrderedEngineSet as _OES


@_ctxlib.contextmanager
def FORI(eng, start, end, step=1):
    b = eng.bass
    engines = _OES([eng.engine])
    loop_id = b.next_id()
    loop_name = f"kfori_{loop_id}"
    loop_start = f"{loop_name}_loop"
    loop_end = f"{loop_name}_end"
    registers = b.alloc_registers(f"{loop_name}_i", engines=engines)
    b.regs_mov(registers, start)
    b.br(loop_start, engines=engines)
    from concourse.bass import sinput_min_value, sinput_max_value
    min_val = sinput_min_value(start)
    max_val = sinput_max_value(end)
    if max_val is not None:
        max_val -= 1
    with b.body(loop_start, valid_engines=engines):
        yield b.snap(registers, min_val=min_val, max_val=max_val)
        b.regs_alu(registers, registers, step, op=mybir.AluOpType.add)
        b.br_lt(registers, end, on_true=loop_start, on_false=loop_end,
                engines=engines)
    b.switch_bb(loop_end)
    for h_ in registers:
        eng.free_register(h_)


def AP(t, off, pattern):
    return bass.AP(t, off, pattern)


def build_program():
    nc = bass.Bass("TRN2", target_bir_lowering=False)

    xb = nc.dram_tensor("xb", [L, E], F32, kind="ExternalInput")
    xq = nc.dram_tensor("xq", [Q, E], F32, kind="ExternalInput")
    lut_in = nc.dram_tensor("lut", [1, 16], F32, kind="ExternalInput")
    gu_idx = nc.dram_tensor("gu_idx", [2 * INTER, E], I32, kind="ExternalInput")
    d_idx = nc.dram_tensor("d_idx", [E, INTER], I32, kind="ExternalInput")
    dmA = nc.dram_tensor("dmA", [128, 4 * 512], BF16, kind="ExternalInput")
    dmB = nc.dram_tensor("dmB", [128, 4 * 512], BF16, kind="ExternalInput")
    out = nc.dram_tensor("o", [Q, E], F32, kind="ExternalOutput")

    xbf = nc.dram_tensor("xbf", [L, E], BF16)
    xqbf = nc.dram_tensor("xqbf", [Q, E], BF16)
    xbfT = nc.dram_tensor("xbfT", [E, L], BF16)
    gub = nc.dram_tensor("gub", [2 * INTER, E], BF16)
    dbf = nc.dram_tensor("dbf", [E, INTER], BF16)
    wgu = nc.dram_tensor("wgu", [2 * E, INTP], BF16)
    wdT = nc.dram_tensor("wdT", [INTP, E], BF16)
    x1d = nc.dram_tensor("x1d", [E, Q], BF16)
    hd = nc.dram_tensor("hd", [44 * 128, Q], BF16)

    ctx = contextlib.ExitStack()
    EE = ctx.enter_context
    with ctx:
        sem = {}
        for n in ["cast", "pre", "cst", "lutld", "cxb", "cxq", "cgu", "cdb", "cpd",
                  "kw", "sc", "ex", "ve", "mk", "pv", "rc", "rb", "ye", "tv",
                  "idx", "bld", "wout",
                  "lnA", "lnB", "lnC", "l2x", "tev", "lnD",
                  "ld2", "f1m", "f1a", "f1b",
                  "ld3", "f2m", "f2e", "l3a", "l3b", "l3c", "l3d", "tv2",
                  "st"]:
            sem[n] = EE(nc.semaphore(n))

        def S(n):
            return sem[n]

        XR = EE(nc.sbuf_tensor("XR", [128, H * Q], BF16))
        lutb = EE(nc.sbuf_tensor("lutb", [128, 16], F32))
        onesb = EE(nc.sbuf_tensor("onesb", [128, 1], BF16))
        onesbf = EE(nc.sbuf_tensor("onesbf", [1, 128], F32))
        onesff = EE(nc.sbuf_tensor("onesff", [128, 1], F32))
        zerot = EE(nc.sbuf_tensor("zerot", [128, 128], BF16))
        ident = EE(nc.sbuf_tensor("ident", [128, 128], BF16))
        identf = EE(nc.sbuf_tensor("identf", [128, 128], F32))
        mskA = EE(nc.sbuf_tensor("mskA", [128, 4 * 512], BF16))
        mskB = EE(nc.sbuf_tensor("mskB", [128, 4 * 512], BF16))
        iotai = EE(nc.sbuf_tensor("iotai", [128, 128], I32))
        eps1 = EE(nc.sbuf_tensor("eps1", [1, 1], F32))
        idxT = EE(nc.sbuf_tensor("idxT", [128, 2 * 2048], BF16))
        wterm = EE(nc.sbuf_tensor("wterm", [128, 2048], BF16))
        wbt = EE(nc.sbuf_tensor("wbt", [128, 2 * 2048], BF16))

        with (
            nc.sbuf_tensor("tpose", [128, L], BF16) as tpose,
            nc.psum_tensor("ps_lut", [128, 16], F32) as ps_lut,
            nc.Block() as blk,
        ):
            @blk.gpsimd
            def _(g):
                for i in range(4):
                    g.dma_start(out=xbf[i * 1024:(i + 1) * 1024, :],
                                in_=xb[i * 1024:(i + 1) * 1024, :]).then_inc(S("cxb"), 16)
                for i in range(2):
                    g.dma_start(out=xqbf[i * 1024:(i + 1) * 1024, :],
                                in_=xq[i * 1024:(i + 1) * 1024, :]).then_inc(S("cxq"), 16)
                for i in range(4):
                    g.dma_start(out=gub[i * 2752:(i + 1) * 2752, :],
                                in_=gu_idx[i * 2752:(i + 1) * 2752, :]).then_inc(S("cgu"), 16)
                for i in range(2):
                    g.dma_start(out=dbf[i * 1024:(i + 1) * 1024, :],
                                in_=d_idx[i * 1024:(i + 1) * 1024, :]).then_inc(S("cdb"), 16)
                g.iota(iotai[:], pattern=[[1, 128]], base=0,
                       channel_multiplier=-1).then_inc(S("cst"), 1)
                g.wait_ge(S("cst"), 2)
                for r in range(32):
                    g.dma_start(out=AP(wgu, r * 128 * INTP + INTER,
                                       [[INTP, 128], [1, 128]]),
                                in_=zerot[:]).then_inc(S("cpd"), 16)
                for cchunk in range(16):
                    g.dma_start(out=AP(wdT, INTER * E + 128 * cchunk,
                                       [[E, 128], [1, 128]]),
                                in_=zerot[:]).then_inc(S("cpd"), 16)

            @blk.sync
            def _(sy):
                sy.dma_start(out=lutb[0:1, :], in_=lut_in[:]).then_inc(S("lutld"), 16)
                sy.dma_start(out=mskA[:], in_=dmA[:]).then_inc(S("pre"), 16)
                sy.dma_start(out=mskB[:], in_=dmB[:]).then_inc(S("pre"), 16)
                sy.wait_ge(S("cxb"), 4 * 16)
                pre = 32
                for e in range(16):
                    sy.wait_ge(S("pre"), pre)
                    sy.dma_start(out=tpose[:], in_=xbf[:, 128 * e:128 * (e + 1)],
                                 transpose=True).then_inc(S("pre"), 16)
                    pre += 16
                    sy.wait_ge(S("pre"), pre)
                    sy.dma_start(out=xbfT[128 * e:128 * (e + 1), :],
                                 in_=tpose[:]).then_inc(S("pre"), 16)
                    pre += 16
                sy.wait_ge(S("cxq"), 2 * 16)
                for h in range(16):
                    sy.dma_start(out=XR[:, Q * h:Q * (h + 1)],
                                 in_=xqbf[:, 128 * h:128 * (h + 1)],
                                 transpose=True).then_inc(S("pre"), 16)

            @blk.vector
            def _(v):
                v.memset(onesb[:], 1.0)
                v.memset(onesbf[:], 1.0)
                v.memset(onesff[:], 1.0)
                v.memset(zerot[:], 0.0)
                v.memset(eps1[:], 1e-5)
                v.wait_ge(S("cst"), 1)
                v.tensor_scalar(ident[:], iotai[:], 0, 1.0,
                                op0=ALU.is_equal, op1=ALU.mult)
                v.tensor_scalar(identf[:], iotai[:], 0, 1.0,
                                op0=ALU.is_equal, op1=ALU.mult).then_inc(S("cst"), 1)

            @blk.tensor
            def _(t):
                t.wait_ge(S("lutld"), 16)
                t.wait_ge(S("cst"), 2)
                t.matmul(ps_lut[:], onesbf[:], lutb[0:1, :],
                         start=True, stop=True).then_inc(S("cst"), 1)

            @blk.vector
            def _(v):
                v.wait_ge(S("cst"), 3)
                v.tensor_copy(lutb[:], ps_lut[:]).then_inc(S("cst"), 1)

        PRE_DONE = 800

        with (
            nc.sbuf_tensor("kw", [128, 512], BF16) as kw,
            nc.sbuf_tensor("vw", [128, 512], BF16) as vw,
            nc.sbuf_tensor("PT", [128, 4 * 512], BF16) as PT,
            nc.sbuf_tensor("rsb", [1, 512], F32) as rsb,
            nc.sbuf_tensor("rt1", [1, 512], F32) as rt1,
            nc.sbuf_tensor("rbs", [128, 2 * 512], F32) as rbs,
            nc.sbuf_tensor("ybuf", [128, 2 * 512], F32) as ybuf,
            nc.sbuf_tensor("tvb", [128, 512], F32) as tvb,
            nc.psum_tensor("ps_s0", [128, 512], F32) as ps_s0,
            nc.psum_tensor("ps_s1", [128, 512], F32) as ps_s1,
            nc.psum_tensor("ps_s2", [128, 512], F32) as ps_s2,
            nc.psum_tensor("ps_s3", [128, 512], F32) as ps_s3,
            nc.psum_tensor("ps_y", [128, 512], F32) as ps_y,
            nc.psum_tensor("ps_sum", [1, 512], F32) as ps_sum,
            nc.psum_tensor("ps_va", [128, 128], BF16) as ps_va,
            nc.psum_tensor("ps_vb", [128, 128], BF16) as ps_vb,
            nc.Block() as blk,
        ):
            ps_s = [ps_s0, ps_s1, ps_s2, ps_s3]

            @blk.scalar
            def _(s):
                s.wait_ge(S("pre"), PRE_DONE)
                s.wait_ge(S("cst"), 4)
                with s.register("rg1") as wc, s.register("rg2") as r1, \
                        s.register("rg3") as r2:
                    s.reg_mov(wc, 0)
                    with FORI(s, 0, 16) as h:
                        for v in range(4):
                            nwin = 2 * v + 2

                            def act_window(wv):
                                s.reg_mul(r1, wc, 2)
                                s.wait_ge(S("sc"), r1)
                                if isinstance(wv, int):
                                    s.reg_mov(r2, 512 * wv)
                                else:
                                    s.reg_mul(r2, wv, 512)
                                s.reg_mul(r1, h, 128 * L)
                                s.reg_add(r1, r1, r2)
                                s.dma_start(out=kw[:],
                                            in_=AP(xbfT, r1, [[L, 128], [1, 512]])
                                            ).then_inc(S("kw"), 16)
                                s.reg_mul(r1, wc, 2)
                                s.reg_add(r1, r1, 1)
                                s.wait_ge(S("sc"), r1)
                                s.activation(vw[:, 0:128], ps_va[:], AF.Copy,
                                             bias=0.0, scale=1.0).then_inc(S("ve"), 1)
                                s.activation(vw[:, 128:256], ps_vb[:], AF.Copy,
                                             bias=0.0, scale=1.0).then_inc(S("ve"), 1)
                                s.reg_mul(r1, wc, 2)
                                s.reg_add(r1, r1, 2)
                                s.wait_ge(S("sc"), r1)
                                s.wait_ge(S("pv"), wc)
                                s.activation(vw[:, 256:384], ps_va[:], AF.Copy,
                                             bias=0.0, scale=1.0).then_inc(S("ve"), 1)
                                s.activation(vw[:, 384:512], ps_vb[:], AF.Copy,
                                             bias=0.0, scale=1.0).then_inc(S("ve"), 1)
                                for j in range(4):
                                    a = s.activation(PT[:, 512 * j:512 * (j + 1)],
                                                     ps_s[j][:], AF.Exp,
                                                     bias=0.0, scale=SCALE)
                                    if j == 3:
                                        a.then_inc(S("ex"), 1)
                                s.reg_add(wc, wc, 1)

                            with FORI(s, 0, nwin) as w:
                                act_window(w)
                            s.wait_ge(S("pv"), wc)
                            s.activation(rt1[:], ps_sum[:], AF.Ln, bias=0.0,
                                         scale=1.0)
                            s.activation(rsb[:], rt1[:], AF.Exp, bias=0.0,
                                         scale=-1.0).then_inc(S("rc"), 1)
                            s.reg_mul(r1, h, 4)
                            s.reg_add(r1, r1, v + 1)
                            s.wait_ge(S("rb"), r1)
                            s.reg_mul(r1, h, 4)
                            s.reg_add(r1, r1, v)
                            s.reg_alu(r1, r1, 1, ALU.subtract)
                            s.reg_alu(r1, r1, 0, ALU.max)
                            s.wait_ge(S("tv"), r1)
                            pr = v % 2
                            s.activation(rbs[:, 512 * pr:512 * (pr + 1)], ps_s0[:],
                                         AF.Copy, bias=0.0, scale=1.0)
                            s.activation(ybuf[:, 512 * pr:512 * (pr + 1)], ps_y[:],
                                         AF.Copy, bias=0.0, scale=1.0).then_inc(S("ye"), 1)

            @blk.tensor
            def _(t):
                with t.register("rg4") as wc, t.register("rg5") as r1, \
                        t.register("rg6") as r2:
                    t.reg_mov(wc, 0)
                    with FORI(t, 0, 16) as h:
                        for v in range(4):
                            nwin = 2 * v + 2
                            t.reg_mul(r1, h, 4)
                            t.reg_add(r1, r1, v)
                            t.wait_ge(S("ye"), r1)
                            t.wait_ge(S("rc"), r1)
                            t.matmul(ps_y[:], zerot[:], mskA[:, 0:512],
                                     start=True, stop=False)
                            t.matmul(ps_sum[:], zerot[:, 0:1], mskA[:, 0:512],
                                     start=True, stop=False)

                            with FORI(t, 0, nwin) as w:
                                t.reg_mul(r1, wc, 16)
                                t.reg_add(r1, r1, 16)
                                t.wait_ge(S("kw"), r1)
                                t.wait_ge(S("ex"), wc)
                                t.reg_mul(r1, wc, 4)
                                t.wait_ge(S("ve"), r1)
                                t.transpose(ps_va[:], kw[:, 0:128], ident[:])
                                t.transpose(ps_vb[:], kw[:, 128:256], ident[:]
                                            ).then_inc(S("sc"), 1)
                                t.reg_mul(r2, h, Q)
                                t.reg_add(r2, r2, 512 * v)
                                xoff = t.snap(r2, min_val=0, max_val=H * Q - 512)
                                for j in range(4):
                                    t.matmul(ps_s[j][:],
                                             kw[:, 128 * j:128 * (j + 1)],
                                             AP(XR, xoff, [[H * Q, 128], [1, 512]]),
                                             start=True, stop=True)
                                t.reg_mul(r1, wc, 4)
                                t.reg_add(r1, r1, 2)
                                t.wait_ge(S("ve"), r1)
                                t.transpose(ps_va[:], kw[:, 256:384], ident[:])
                                t.transpose(ps_vb[:], kw[:, 384:512], ident[:]
                                            ).then_inc(S("sc"), 1)
                                t.reg_add(r1, wc, 1)
                                t.wait_ge(S("ex"), r1)
                                t.reg_mul(r1, h, 8)
                                t.reg_add(r1, r1, w)
                                t.reg_add(r1, r1, 1)
                                t.wait_ge(S("mk"), r1)
                                for j in range(4):
                                    t.matmul(ps_y[:],
                                             vw[:, 128 * j:128 * (j + 1)],
                                             PT[:, 512 * j:512 * (j + 1)],
                                             start=False, stop=False)
                                for j in range(4):
                                    mm = t.matmul(ps_sum[:], onesb[:],
                                                  PT[:, 512 * j:512 * (j + 1)],
                                                  start=False, stop=False)
                                mm.then_inc(S("pv"), 1)
                                t.reg_add(wc, wc, 1)
                            t.matmul(ps_y[:], zerot[:], mskA[:, 0:512],
                                     start=False, stop=True)
                            t.matmul(ps_sum[:], zerot[:, 0:1], mskA[:, 0:512],
                                     start=False, stop=True)
                            t.reg_mul(r1, h, 4)
                            t.reg_add(r1, r1, v + 1)
                            t.wait_ge(S("rc"), r1)
                            t.wait_ge(S("ex"), wc)
                            t.matmul(ps_s0[:], onesbf[:], rsb[:],
                                     start=True, stop=True).then_inc(S("rb"), 1)

            @blk.gpsimd
            def _(g):
                with g.register("rg7") as r1, g.register("rg8") as r2:
                    with FORI(g, 0, 16) as h:
                        for v in range(4):
                            nwin = 2 * v + 2
                            base = v * v + v
                            for ki, msk in ((0, mskA), (1, mskB)):
                                wce = base + nwin - 2 + ki
                                g.reg_mul(r1, h, 20)
                                g.reg_add(r1, r1, wce + 1)
                                g.wait_ge(S("ex"), r1)
                                for j in range(4):
                                    tt = g.tensor_tensor(PT[:, 512 * j:512 * (j + 1)],
                                                         PT[:, 512 * j:512 * (j + 1)],
                                                         msk[:, 512 * j:512 * (j + 1)],
                                                         op=ALU.mult)
                                tt.then_inc(S("mk"), 1)
                            g.reg_mul(r1, h, 4)
                            g.reg_add(r1, r1, v + 1)
                            g.wait_ge(S("ye"), r1)
                            pr = v % 2
                            g.tensor_tensor(tvb[:], ybuf[:, 512 * pr:512 * (pr + 1)],
                                            rbs[:, 512 * pr:512 * (pr + 1)],
                                            op=ALU.mult)
                            g.reg_mul(r2, h, Q)
                            g.reg_add(r2, r2, 512 * v)
                            goff = g.snap(r2, min_val=0, max_val=H * Q - 512)
                            g.tensor_tensor(AP(XR, goff, [[H * Q, 128], [1, 512]]),
                                            AP(XR, goff, [[H * Q, 128], [1, 512]]),
                                            tvb[:], op=ALU.add).then_inc(S("tv"), 1)

            @blk.vector
            def _(v):
                v.wait_ge(S("cst"), 4)
                with v.register("rg9") as r1:
                    with FORI(v, 0, NGU // 2) as i:
                        for par in range(2):
                            v.reg_mul(r1, i, 32)
                            v.reg_add(r1, r1, 16 * par + 16)
                            v.wait_ge(S("idx"), r1)
                            v.reg_mul(r1, i, 32)
                            if par == 0:
                                v.reg_alu(r1, r1, 16, ALU.subtract)
                                v.reg_alu(r1, r1, 0, ALU.max)
                            v.wait_ge(S("wout"), r1)
                            sl = 2048 * par
                            v.memset(wbt[:, sl:sl + SLW], 0.0)
                            for k in range(16):
                                v.tensor_scalar(wterm[:, 0:SLW],
                                                idxT[:, sl:sl + SLW], float(k),
                                                lutb[:, k:k + 1],
                                                op0=ALU.is_equal, op1=ALU.mult)
                                tt = v.tensor_tensor(wbt[:, sl:sl + SLW],
                                                     wbt[:, sl:sl + SLW],
                                                     wterm[:, 0:SLW], op=ALU.add)
                            tt.then_inc(S("bld"), 1)

            @blk.sync
            def _(sy):
                sy.wait_ge(S("cgu"), 4 * 16)
                with sy.register("rg10") as r1, sy.register("rg11") as r2, \
                        sy.register("rg12") as rT:
                    with FORI(sy, 0, NGU // 2) as i:
                        for par in range(2):
                            sy.reg_mul(rT, i, 2)
                            sy.reg_add(rT, rT, par)
                            sy.reg_alu(r1, rT, 1, ALU.subtract)
                            sy.reg_alu(r1, r1, 0, ALU.max)
                            sy.wait_ge(S("bld"), r1)
                            sy.reg_div(r1, rT, 32)
                            sy.reg_mul(r1, r1, SLW)
                            sy.reg_mod(r2, rT, 32)
                            sy.reg_div(r2, r2, 16)
                            sy.reg_mul(r2, r2, INTER)
                            sy.reg_add(r1, r1, r2)
                            sy.reg_mul(r1, r1, E)
                            sy.reg_mod(r2, rT, 16)
                            sy.reg_mul(r2, r2, 128)
                            sy.reg_add(r1, r1, r2)
                            sy.dma_start(out=idxT[:, 2048 * par:2048 * par + SLW],
                                         in_=AP(gub, r1, [[E, SLW], [1, 128]]),
                                         transpose=True).then_inc(S("idx"), 16)
                            sy.reg_add(r1, rT, 1)
                            sy.wait_ge(S("bld"), r1)
                            sy.reg_mod(r1, rT, 32)
                            sy.reg_mul(r1, r1, 128 * INTP)
                            sy.reg_div(r2, rT, 32)
                            sy.reg_mul(r2, r2, SLW)
                            sy.reg_add(r1, r1, r2)
                            sy.dma_start(out=AP(wgu, r1, [[INTP, 128], [1, SLW]]),
                                         in_=wbt[:, 2048 * par:2048 * par + SLW]
                                         ).then_inc(S("wout"), 16)

        with (
            nc.sbuf_tensor("sqf", [128, 2 * 512], F32) as sqf,
            nc.sbuf_tensor("mrow", [1, 512], F32) as mrow,
            nc.sbuf_tensor("qrow", [1, 512], F32) as qrow,
            nc.sbuf_tensor("m2row", [1, 512], F32) as m2row,
            nc.sbuf_tensor("vrow", [1, 512], F32) as vrow,
            nc.sbuf_tensor("rrow", [1, 512], F32) as rrow,
            nc.sbuf_tensor("Mb", [128, 512], F32) as Mb,
            nc.sbuf_tensor("Rb", [128, 512], F32) as Rb,
            nc.sbuf_tensor("ntf", [128, 512], F32) as ntf,
            nc.sbuf_tensor("xt1t", [128, 512], BF16) as xt1t,
            nc.psum_tensor("ps_st", [1, 512], F32) as ps_st,
            nc.psum_tensor("ps_sq", [1, 512], F32) as ps_sq,
            nc.psum_tensor("ps_bc", [128, 512], F32) as ps_bc,
            nc.Block() as blk,
        ):
            @blk.scalar
            def _(s):
                s.wait_ge(S("tv"), 64)
                with s.register("sA") as r1, s.register("sB") as r2, \
                        s.register("sC") as r3:
                    with FORI(s, 0, 4) as qb:
                        with FORI(s, 0, 16) as hh:
                            s.reg_alu(r1, hh, 1, ALU.subtract)
                            s.reg_alu(r1, r1, 0, ALU.max)
                            s.reg_mul(r3, qb, 17)
                            s.reg_add(r1, r1, r3)
                            s.wait_ge(S("lnB"), r1)
                            s.reg_mod(r2, hh, 2)
                            s.reg_mul(r2, r2, 512)
                            soff2 = s.snap(r2, min_val=0, max_val=512)
                            s.reg_mul(r1, hh, Q)
                            s.reg_mul(r3, qb, 512)
                            s.reg_add(r1, r1, r3)
                            soff1 = s.snap(r1, min_val=0, max_val=H * Q - 512)
                            s.activation(AP(sqf, soff2, [[1024, 128], [1, 512]]),
                                         AP(XR, soff1, [[H * Q, 128], [1, 512]]),
                                         AF.Square, bias=0.0, scale=1.0
                                         ).then_inc(S("lnA"), 1)
                        s.reg_mul(r1, qb, 17)
                        s.reg_add(r1, r1, 17)
                        s.wait_ge(S("lnB"), r1)
                        s.activation(mrow[:], ps_st[:], AF.Copy, bias=0.0,
                                     scale=1.0 / E)
                        s.activation(qrow[:], ps_sq[:], AF.Copy, bias=0.0,
                                     scale=1.0 / E).then_inc(S("l2x"), 1)
                        s.reg_add(r1, qb, 1)
                        s.wait_ge(S("lnC"), r1)
                        s.activation(vrow[:], vrow[:], AF.Sqrt, bias=eps1[:],
                                     scale=1.0)
                        s.activation(vrow[:], vrow[:], AF.Ln, bias=0.0, scale=1.0)
                        s.activation(rrow[:], vrow[:], AF.Exp, bias=0.0,
                                     scale=-1.0).then_inc(S("l2x"), 1)
                        s.reg_mul(r1, qb, 2)
                        s.reg_add(r1, r1, 1)
                        s.wait_ge(S("tev"), r1)
                        s.activation(Mb[:], ps_bc[:], AF.Copy, bias=0.0,
                                     scale=1.0).then_inc(S("l2x"), 1)
                        s.reg_mul(r1, qb, 2)
                        s.reg_add(r1, r1, 2)
                        s.wait_ge(S("tev"), r1)
                        s.activation(Rb[:], ps_bc[:], AF.Copy, bias=0.0,
                                     scale=1.0).then_inc(S("l2x"), 1)

            @blk.tensor
            def _(t):
                with t.register("tA") as r1, t.register("tB") as r2, \
                        t.register("tC") as r3:
                    t.wait_ge(S("tv"), 64)
                    with FORI(t, 0, 4) as qb:
                        t.reg_mul(r1, qb, 4)
                        t.reg_alu(r1, r1, 3, ALU.subtract)
                        t.reg_alu(r1, r1, 0, ALU.max)
                        t.wait_ge(S("l2x"), r1)
                        t.matmul(ps_st[:], zerot[:, 0:1], mskA[:, 0:512],
                                 start=True, stop=False)
                        t.matmul(ps_sq[:], zerot[:, 0:1], mskA[:, 0:512],
                                 start=True, stop=False)
                        with FORI(t, 0, 16) as hh:
                            t.reg_mul(r1, hh, Q)
                            t.reg_mul(r3, qb, 512)
                            t.reg_add(r1, r1, r3)
                            toff1 = t.snap(r1, min_val=0, max_val=H * Q - 512)
                            t.matmul(ps_st[:], onesb[:],
                                     AP(XR, toff1, [[H * Q, 128], [1, 512]]),
                                     start=False, stop=False)
                            t.reg_mul(r3, qb, 16)
                            t.reg_add(r1, hh, 1)
                            t.reg_alu(r1, r1, r3, ALU.add)
                            t.wait_ge(S("lnA"), r1)
                            t.reg_mod(r2, hh, 2)
                            t.reg_mul(r2, r2, 512)
                            toff2 = t.snap(r2, min_val=0, max_val=512)
                            t.matmul(ps_sq[:], onesff[:],
                                     AP(sqf, toff2, [[1024, 128], [1, 512]]),
                                     start=False, stop=False).then_inc(S("lnB"), 1)
                        t.matmul(ps_st[:], zerot[:, 0:1], mskA[:, 0:512],
                                 start=False, stop=True)
                        t.matmul(ps_sq[:], zerot[:, 0:1], mskA[:, 0:512],
                                 start=False, stop=True).then_inc(S("lnB"), 1)
                        t.reg_mul(r1, qb, 4)
                        t.reg_add(r1, r1, 1)
                        t.wait_ge(S("l2x"), r1)
                        t.matmul(ps_bc[:], onesbf[:], mrow[:],
                                 start=True, stop=True).then_inc(S("tev"), 1)
                        t.reg_mul(r1, qb, 4)
                        t.reg_add(r1, r1, 3)
                        t.wait_ge(S("l2x"), r1)
                        t.matmul(ps_bc[:], onesbf[:], rrow[:],
                                 start=True, stop=True).then_inc(S("tev"), 1)

            @blk.gpsimd
            def _(g):
                with g.register("gA") as r1, g.register("gB") as r2:
                    with FORI(g, 0, 4) as qb:
                        g.reg_mul(r1, qb, 4)
                        g.reg_add(r1, r1, 1)
                        g.wait_ge(S("l2x"), r1)
                        g.tensor_tensor(m2row[:], mrow[:], mrow[:], op=ALU.mult)
                        g.tensor_tensor(vrow[:], qrow[:], m2row[:],
                                        op=ALU.subtract).then_inc(S("lnC"), 1)
                        g.reg_mul(r1, qb, 4)
                        g.reg_add(r1, r1, 4)
                        g.wait_ge(S("l2x"), r1)
                        with FORI(g, 0, 16) as hh:
                            g.reg_mul(r1, hh, Q)
                            g.reg_mul(r2, qb, 512)
                            g.reg_add(r1, r1, r2)
                            goff1 = g.snap(r1, min_val=0, max_val=H * Q - 512)
                            g.tensor_tensor(ntf[:],
                                            AP(XR, goff1, [[H * Q, 128], [1, 512]]),
                                            Mb[:], op=ALU.subtract)
                            g.tensor_tensor(xt1t[:], ntf[:], Rb[:], op=ALU.mult)
                            g.reg_mul(r1, hh, 128 * Q)
                            g.reg_mul(r2, qb, 512)
                            g.reg_add(r1, r1, r2)
                            g.dma_start(out=AP(x1d, r1, [[Q, 128], [1, 512]]),
                                        in_=xt1t[:]).then_inc(S("lnD"), 16)
                            g.reg_mul(r1, hh, 16)
                            g.reg_mul(r2, qb, 256)
                            g.reg_add(r1, r1, r2)
                            g.reg_add(r1, r1, 16)
                            g.wait_ge(S("lnD"), r1)

            @blk.vector
            def _(v):
                with v.register("rg18") as r1:
                    with FORI(v, 0, 44) as T2:
                        v.reg_mul(r1, T2, 16)
                        v.reg_add(r1, r1, NGU * 16 + 16)
                        v.wait_ge(S("idx"), r1)
                        v.reg_mul(r1, T2, 16)
                        v.reg_add(r1, r1, NGU * 16)
                        v.wait_ge(S("wout"), r1)
                        sl = 0
                        v.memset(wbt[:, sl:sl + 2048], 0.0)
                        for k in range(16):
                            v.tensor_scalar(wterm[:, 0:2048],
                                            idxT[:, sl:sl + 2048], float(k),
                                            lutb[:, k:k + 1],
                                            op0=ALU.is_equal, op1=ALU.mult)
                            tt = v.tensor_tensor(wbt[:, sl:sl + 2048],
                                                 wbt[:, sl:sl + 2048],
                                                 wterm[:, 0:2048], op=ALU.add)
                        tt.then_inc(S("bld"), 1)

            @blk.sync
            def _(sy):
                sy.wait_ge(S("cdb"), 2 * 16)
                with sy.register("rg19") as r1, sy.register("rg20") as r2:
                    with FORI(sy, 0, 44) as T2:
                        sy.reg_alu(r2, T2, 42, ALU.min)
                        sy.reg_add(r1, T2, NGU)
                        sy.wait_ge(S("bld"), r1)
                        sy.reg_mul(r1, r2, 128)
                        sy.dma_start(out=idxT[:, 0:2048],
                                     in_=AP(dbf, r1, [[INTER, 2048], [1, 128]]),
                                     transpose=True).then_inc(S("idx"), 16)
                        sy.reg_add(r1, T2, NGU + 1)
                        sy.wait_ge(S("bld"), r1)
                        sy.reg_mul(r1, r2, 128 * E)
                        sy.dma_start(out=AP(wdT, r1, [[E, 128], [1, 2048]]),
                                     in_=wbt[:, 0:2048]).then_inc(S("wout"), 16)

        with (
            nc.sbuf_tensor("XT1", [128, 16 * 512], BF16) as XT1,
            nc.sbuf_tensor("Wgs", [128, 8192], BF16) as Wgs,
            nc.sbuf_tensor("Wus", [128, 8192], BF16) as Wus,
            nc.sbuf_tensor("gsil", [128, 2 * 512], BF16) as gsil,
            nc.sbuf_tensor("ubuf", [128, 2 * 512], F32) as ubuf,
            nc.sbuf_tensor("hq", [128, 512], BF16) as hq,
            nc.psum_tensor("ps_g0", [128, 512], F32) as ps_g0,
            nc.psum_tensor("ps_u0", [128, 512], F32) as ps_u0,
            nc.psum_tensor("ps_g1", [128, 512], F32) as ps_g1,
            nc.psum_tensor("ps_u1", [128, 512], F32) as ps_u1,
            nc.Block() as blk,
        ):
            ps_g = [ps_g0, ps_g1]
            ps_u = [ps_u0, ps_u1]
            @blk.sync
            def _(sy):
                sy.wait_ge(S("lnD"), 16 * 64)
                sy.wait_ge(S("cpd"), 48 * 16)
                with sy.register("rg21") as r1, sy.register("rg22") as r2:
                    with FORI(sy, 0, 4) as tp:
                        sy.reg_mul(r1, tp, 44)
                        sy.wait_ge(S("f1m"), r1)
                        sy.reg_mul(r1, tp, 512)
                        sy.dma_start(out=XT1[:, 0:8 * 512],
                                     in_=AP(x1d, r1, [[Q, 128], [128 * Q, 8], [1, 512]])
                                     ).then_inc(S("ld2"), 8 * 0 + 16)
                        sy.reg_mul(r1, tp, 512)
                        sy.reg_add(r1, r1, 8 * 128 * Q)
                        sy.dma_start(out=XT1[:, 8 * 512:16 * 512],
                                     in_=AP(x1d, r1, [[Q, 128], [128 * Q, 8], [1, 512]])
                                     ).then_inc(S("ld2"), 16)
                        with FORI(sy, 0, 11) as icg:
                            sy.reg_mul(r2, icg, 512)
                            sy.reg_add(r2, r2, 512 + SLW - 1)
                            sy.reg_div(r2, r2, SLW)
                            sy.reg_alu(r2, r2, 4, ALU.min)
                            sy.reg_mul(r2, r2, 32 * 16)
                            sy.wait_ge(S("wout"), r2)
                            sy.reg_mul(r1, tp, 44)
                            sy.reg_mul(r2, icg, 4)
                            sy.reg_add(r1, r1, r2)
                            sy.wait_ge(S("f1m"), r1)
                            sy.reg_mul(r2, icg, 512)
                            sy.dma_start(
                                out=Wgs[:, 0:8 * 512],
                                in_=AP(wgu, r2, [[INTP, 128], [128 * INTP, 8], [1, 512]])
                                ).then_inc(S("ld2"), 16)
                            sy.reg_add(r2, r2, 8 * 128 * INTP)
                            sy.dma_start(
                                out=Wgs[:, 8 * 512:16 * 512],
                                in_=AP(wgu, r2, [[INTP, 128], [128 * INTP, 8], [1, 512]])
                                ).then_inc(S("ld2"), 16)
                            sy.reg_mul(r2, icg, 512)
                            sy.reg_add(r2, r2, E * INTP)
                            sy.dma_start(
                                out=Wus[:, 0:8 * 512],
                                in_=AP(wgu, r2, [[INTP, 128], [128 * INTP, 8], [1, 512]])
                                ).then_inc(S("ld2"), 16)
                            sy.reg_add(r2, r2, 8 * 128 * INTP)
                            sy.dma_start(
                                out=Wus[:, 8 * 512:16 * 512],
                                in_=AP(wgu, r2, [[INTP, 128], [128 * INTP, 8], [1, 512]])
                                ).then_inc(S("ld2"), 16)

            @blk.tensor
            def _(t):
                with t.register("rg23") as r1, t.register("rg24") as rc2, \
                        t.register("rg25") as r2:
                    t.reg_mov(rc2, 0)
                    with FORI(t, 0, 4) as tp:
                        with FORI(t, 0, 11) as icg:
                            t.reg_mul(r1, tp, 11)
                            t.reg_add(r1, r1, icg)
                            t.reg_mul(r1, r1, 64)
                            t.reg_add(r1, r1, 64)
                            t.reg_mul(r2, tp, 32)
                            t.reg_add(r1, r1, r2)
                            t.reg_add(r1, r1, 32)
                            t.wait_ge(S("ld2"), r1)
                            for ic in range(4):
                                par = ic % 2
                                t.reg_alu(r1, rc2, 1, ALU.subtract)
                                t.reg_alu(r1, r1, 0, ALU.max)
                                t.reg_mul(r1, r1, 16)
                                t.wait_ge(S("f1b"), r1)
                                for hc in range(16):
                                    t.matmul(ps_g[par][:],
                                             Wgs[:, 512 * hc + 128 * ic:
                                                 512 * hc + 128 * (ic + 1)],
                                             XT1[:, 512 * hc:512 * (hc + 1)],
                                             start=(hc == 0), stop=(hc == 15))
                                for hc in range(16):
                                    mm = t.matmul(ps_u[par][:],
                                                  Wus[:, 512 * hc + 128 * ic:
                                                      512 * hc + 128 * (ic + 1)],
                                                  XT1[:, 512 * hc:512 * (hc + 1)],
                                                  start=(hc == 0), stop=(hc == 15))
                                mm.then_inc(S("f1m"), 1)
                                t.reg_add(rc2, rc2, 1)

            @blk.scalar
            def _(s):
                with s.register("rg26") as r1:
                    with FORI(s, 0, 88) as i2:
                        for par in range(2):
                            s.reg_mul(r1, i2, 2)
                            s.reg_add(r1, r1, par + 1)
                            s.wait_ge(S("f1m"), r1)
                            s.reg_mul(r1, i2, 2)
                            s.reg_add(r1, r1, par)
                            s.reg_alu(r1, r1, 1, ALU.subtract)
                            s.reg_alu(r1, r1, 0, ALU.max)
                            s.reg_mul(r1, r1, 16)
                            s.wait_ge(S("f1b"), r1)
                            s.activation(gsil[:, 512 * par:512 * (par + 1)],
                                         ps_g[par][:], AF.Silu, bias=0.0,
                                         scale=1.0)
                            s.activation(ubuf[:, 512 * par:512 * (par + 1)],
                                         ps_u[par][:], AF.Copy, bias=0.0,
                                         scale=1.0).then_inc(S("f1a"), 1)

            @blk.gpsimd
            def _(g):
                with g.register("rg27") as r1, g.register("rg28") as r2:
                    with FORI(g, 0, 88) as i2:
                        for par in range(2):
                            g.reg_mul(r1, i2, 2)
                            g.reg_add(r1, r1, par + 1)
                            g.wait_ge(S("f1a"), r1)
                            g.tensor_tensor(hq[:],
                                            gsil[:, 512 * par:512 * (par + 1)],
                                            ubuf[:, 512 * par:512 * (par + 1)],
                                            op=ALU.mult)
                            g.reg_mul(r1, i2, 2)
                            g.reg_add(r1, r1, par)
                            g.reg_div(r2, r1, 44)
                            g.reg_mod(r1, r1, 44)
                            g.reg_mul(r1, r1, 128 * Q)
                            g.reg_mul(r2, r2, 512)
                            g.reg_add(r1, r1, r2)
                            g.dma_start(out=AP(hd, r1, [[Q, 128], [1, 512]]),
                                        in_=hq[:]).then_inc(S("f1b"), 16)
                            g.reg_mul(r1, i2, 2)
                            g.reg_add(r1, r1, par + 1)
                            g.reg_mul(r1, r1, 16)
                            g.wait_ge(S("f1b"), r1)

        with contextlib.ExitStack() as ph4ctx:
            P4 = ph4ctx.enter_context
            XT1b = P4(nc.sbuf_tensor("XT1b", [128, 16 * 512], BF16))
            Wds = P4(nc.sbuf_tensor("Wds", [128, 4 * 512], BF16))
            hs = P4(nc.sbuf_tensor("hs", [128, 4 * 512], BF16))
            dbuf = P4(nc.sbuf_tensor("dbuf", [128, 4 * 512], F32))
            x2p = P4(nc.sbuf_tensor("x2p", [128, 16 * 512], F32))
            sq2 = P4(nc.sbuf_tensor("sq2", [128, 2 * 512], F32))
            mrow2 = P4(nc.sbuf_tensor("mrow2", [1, 512], F32))
            qrow2 = P4(nc.sbuf_tensor("qrow2", [1, 512], F32))
            m2r2 = P4(nc.sbuf_tensor("m2r2", [1, 512], F32))
            vrow2 = P4(nc.sbuf_tensor("vrow2", [1, 512], F32))
            rrow2 = P4(nc.sbuf_tensor("rrow2", [1, 512], F32))
            Mb2 = P4(nc.sbuf_tensor("Mb2", [128, 512], F32))
            Rb2 = P4(nc.sbuf_tensor("Rb2", [128, 512], F32))
            ntf2 = P4(nc.sbuf_tensor("ntf2", [128, 512], F32))
            x2tm = P4(nc.sbuf_tensor("x2tm", [128, 2048], F32))
            ps_d0 = P4(nc.psum_tensor("ps_d0", [128, 512], F32))
            ps_d1 = P4(nc.psum_tensor("ps_d1", [128, 512], F32))
            ps_d2 = P4(nc.psum_tensor("ps_d2", [128, 512], F32))
            ps_d3 = P4(nc.psum_tensor("ps_d3", [128, 512], F32))
            ps_t = P4(nc.psum_tensor("ps_t", [128, 128], F32))
            ps_s2r = P4(nc.psum_tensor("ps_s2r", [1, 512], F32))
            ps_q2r = P4(nc.psum_tensor("ps_q2r", [1, 512], F32))
            blk = P4(nc.Block())

            ps_d = [ps_d0, ps_d1, ps_d2, ps_d3]

            @blk.sync
            def _(sy):
                sy.wait_ge(S("f1b"), 176 * 16)
                sy.wait_ge(S("cpd"), 48 * 16)
                with sy.register("rg29") as r1, sy.register("rg30") as r2, \
                        sy.register("rg31") as r3:
                    with FORI(sy, 0, 4) as tp:
                        sy.reg_mul(r1, tp, 16)
                        sy.wait_ge(S("l3a"), r1)
                        sy.reg_mul(r1, tp, 512)
                        sy.dma_start(out=XT1b[:, 0:8 * 512],
                                     in_=AP(x1d, r1, [[Q, 128], [128 * Q, 8], [1, 512]])
                                     ).then_inc(S("ld3"), 16)
                        sy.reg_mul(r1, tp, 512)
                        sy.reg_add(r1, r1, 8 * 128 * Q)
                        sy.dma_start(out=XT1b[:, 8 * 512:16 * 512],
                                     in_=AP(x1d, r1, [[Q, 128], [128 * Q, 8], [1, 512]])
                                     ).then_inc(S("ld3"), 16)
                        with FORI(sy, 0, 4) as ecb:
                            with FORI(sy, 0, 11) as icg:
                                sy.reg_mul(r1, icg, 4)
                                sy.reg_add(r1, r1, 4)
                                sy.reg_alu(r1, r1, 43, ALU.min)
                                sy.reg_add(r1, r1, NGU)
                                sy.reg_mul(r1, r1, 16)
                                sy.wait_ge(S("wout"), r1)
                                sy.reg_mul(r1, tp, 112)
                                sy.reg_mul(r2, ecb, 12)
                                sy.reg_add(r1, r1, r2)
                                sy.reg_add(r1, r1, icg)
                                sy.wait_ge(S("f2m"), r1)
                                sy.reg_mul(r2, icg, 512 * E)
                                sy.reg_mul(r3, ecb, 512)
                                sy.reg_add(r2, r2, r3)
                                sy.dma_start(
                                    out=Wds[:],
                                    in_=AP(wdT, r2, [[E, 128], [128 * E, 4], [1, 512]])
                                    ).then_inc(S("ld3"), 16)
                                sy.reg_mul(r2, icg, 512 * Q)
                                sy.reg_mul(r3, tp, 512)
                                sy.reg_add(r2, r2, r3)
                                sy.dma_start(
                                    out=hs[:],
                                    in_=AP(hd, r2, [[Q, 128], [128 * Q, 4], [1, 512]])
                                    ).then_inc(S("ld3"), 16)

            @blk.tensor
            def _(t):
                with t.register("rg32") as r1, t.register("rg33") as r2:
                    with FORI(t, 0, 4) as tp:
                        with FORI(t, 0, 4) as ecb:
                            t.reg_mul(r1, tp, 4)
                            t.wait_ge(S("l3d"), r1)
                            t.reg_mul(r1, tp, 16)
                            t.reg_mul(r2, ecb, 4)
                            t.reg_add(r1, r1, r2)
                            t.wait_ge(S("f2e"), r1)
                            for j in range(4):
                                t.matmul(ps_d[j][:], zerot[:], mskA[:, 0:512],
                                         start=True, stop=False)
                            with FORI(t, 0, 11) as icg:
                                t.reg_mul(r1, tp, 90)
                                t.reg_mul(r2, ecb, 11)
                                t.reg_add(r2, r2, icg)
                                t.reg_mul(r2, r2, 2)
                                t.reg_add(r1, r1, r2)
                                t.reg_add(r1, r1, 4)
                                t.reg_mul(r1, r1, 16)
                                t.wait_ge(S("ld3"), r1)
                                for sblk in range(4):
                                    for j in range(4):
                                        mm = t.matmul(ps_d[j][:],
                                                 Wds[:, 512 * sblk + 128 * j:
                                                     512 * sblk + 128 * (j + 1)],
                                                 hs[:, 512 * sblk:512 * (sblk + 1)],
                                                 start=False, stop=False)
                                mm.then_inc(S("f2m"), 1)
                            for j in range(4):
                                mm = t.matmul(ps_d[j][:], zerot[:], mskA[:, 0:512],
                                              start=False, stop=True)
                            mm.then_inc(S("f2m"), 1)
                        t.reg_mul(r1, tp, 16)
                        t.reg_add(r1, r1, 16)
                        t.wait_ge(S("l3a"), r1)
                        if True:
                            t.reg_mul(r1, tp, 4)
                            t.wait_ge(S("l3d"), r1)
                        t.matmul(ps_s2r[:], zerot[:, 0:1], mskA[:, 0:512],
                                 start=True, stop=False)
                        t.matmul(ps_q2r[:], zerot[:, 0:1], mskA[:, 0:512],
                                 start=True, stop=False)
                        with FORI(t, 0, 16) as ee:
                            t.reg_mul(r1, ee, 512)
                            toff3 = t.snap(r1, min_val=0, max_val=16 * 512 - 512)
                            t.matmul(ps_s2r[:], onesff[:],
                                     AP(x2p, toff3, [[16 * 512, 128], [1, 512]]),
                                     start=False, stop=False)
                            t.reg_mul(r1, tp, 80)
                            t.reg_add(r1, r1, ee)
                            t.reg_add(r1, r1, 1)
                            t.wait_ge(S("l3b"), r1)
                            t.reg_mod(r2, ee, 2)
                            t.reg_mul(r2, r2, 512)
                            toff4 = t.snap(r2, min_val=0, max_val=512)
                            t.matmul(ps_q2r[:], onesff[:],
                                     AP(sq2, toff4, [[1024, 128], [1, 512]]),
                                     start=False, stop=False).then_inc(S("l3c"), 1)
                        t.matmul(ps_s2r[:], zerot[:, 0:1], mskA[:, 0:512],
                                 start=False, stop=True)
                        t.matmul(ps_q2r[:], zerot[:, 0:1], mskA[:, 0:512],
                                 start=False, stop=True).then_inc(S("l3c"), 1)
                        t.reg_mul(r1, tp, 4)
                        t.reg_add(r1, r1, 1)
                        t.wait_ge(S("l3d"), r1)
                        t.reg_mul(r1, tp, 4)
                        t.reg_add(r1, r1, 2)
                        t.wait_ge(S("l3d"), r1)
                        t.matmul(ps_d0[:], onesbf[:], mrow2[:],
                                 start=True, stop=True).then_inc(S("tv2"), 1)
                        t.reg_mul(r1, tp, 4)
                        t.reg_add(r1, r1, 3)
                        t.wait_ge(S("l3d"), r1)
                        t.matmul(ps_d0[:], onesbf[:], rrow2[:],
                                 start=True, stop=True).then_inc(S("tv2"), 1)
                        t.reg_add(r1, tp, 64 + 1)
                        t.wait_ge(S("tv"), r1)
                        for qs in range(4):
                            for ec in range(16):
                                k = 16 * qs + ec
                                t.reg_mul(r2, tp, 80)
                                t.reg_add(r2, r2, 16 + k)
                                t.wait_ge(S("l3b"), r2)
                                t.transpose(ps_t[:],
                                            x2p[:, 512 * ec + 128 * qs:
                                                512 * ec + 128 * (qs + 1)],
                                            identf[:]).then_inc(S("f2m"), 1)

            @blk.scalar
            def _(s):
                with s.register("rg34") as r1, s.register("rg35") as r2:
                    with FORI(s, 0, 4) as tp:
                        with FORI(s, 0, 4) as ecb:
                            s.reg_mul(r1, tp, 48 + 64)
                            s.reg_mul(r2, ecb, 12)
                            s.reg_add(r1, r1, r2)
                            s.reg_add(r1, r1, 12)
                            s.wait_ge(S("f2m"), r1)
                            s.reg_mul(r1, tp, 16)
                            s.reg_mul(r2, ecb, 4)
                            s.reg_add(r1, r1, r2)
                            s.wait_ge(S("l3a"), r1)
                            for j in range(4):
                                s.activation(dbuf[:, 512 * j:512 * (j + 1)],
                                             ps_d[j][:], AF.Copy, bias=0.0,
                                             scale=1.0).then_inc(S("f2e"), 1)
                        s.reg_mul(r1, tp, 16)
                        s.reg_add(r1, r1, 16)
                        s.wait_ge(S("l3a"), r1)
                        with FORI(s, 0, 16) as ee:
                            s.reg_mul(r1, tp, 17)
                            s.reg_add(r1, r1, ee)
                            s.reg_alu(r1, r1, 1, ALU.subtract)
                            s.reg_alu(r1, r1, 0, ALU.max)
                            s.wait_ge(S("l3c"), r1)
                            s.reg_mod(r2, ee, 2)
                            s.reg_mul(r2, r2, 512)
                            soff3 = s.snap(r2, min_val=0, max_val=512)
                            s.reg_mul(r1, ee, 512)
                            soff4 = s.snap(r1, min_val=0, max_val=16 * 512 - 512)
                            s.activation(AP(sq2, soff3, [[1024, 128], [1, 512]]),
                                         AP(x2p, soff4, [[16 * 512, 128], [1, 512]]),
                                         AF.Square, bias=0.0, scale=1.0
                                         ).then_inc(S("l3b"), 1)
                        s.reg_mul(r1, tp, 17)
                        s.reg_add(r1, r1, 17)
                        s.wait_ge(S("l3c"), r1)
                        s.activation(mrow2[:], ps_s2r[:], AF.Copy, bias=0.0,
                                     scale=1.0 / E)
                        s.activation(qrow2[:], ps_q2r[:], AF.Copy, bias=0.0,
                                     scale=1.0 / E).then_inc(S("l3d"), 1)
                        s.reg_add(r1, tp, 4 + 1)
                        s.wait_ge(S("lnC"), r1)
                        s.activation(vrow2[:], vrow2[:], AF.Sqrt, bias=eps1[:],
                                     scale=1.0)
                        s.activation(vrow2[:], vrow2[:], AF.Ln, bias=0.0,
                                     scale=1.0)
                        s.activation(rrow2[:], vrow2[:], AF.Exp, bias=0.0,
                                     scale=-1.0).then_inc(S("l3d"), 1)
                        s.reg_mul(r1, tp, 2)
                        s.reg_add(r1, r1, 1)
                        s.wait_ge(S("tv2"), r1)
                        s.activation(Mb2[:], ps_d0[:], AF.Copy, bias=0.0,
                                     scale=1.0).then_inc(S("l3d"), 1)
                        s.reg_mul(r1, tp, 2)
                        s.reg_add(r1, r1, 2)
                        s.wait_ge(S("tv2"), r1)
                        s.activation(Rb2[:], ps_d0[:], AF.Copy, bias=0.0,
                                     scale=1.0).then_inc(S("l3d"), 1)
                        for qs in range(4):
                            for ec in range(16):
                                k = 16 * qs + ec
                                s.reg_mul(r1, tp, 112)
                                s.reg_add(r1, r1, 48 + k + 1)
                                s.wait_ge(S("f2m"), r1)
                                s.activation(x2tm[:, 128 * ec:128 * (ec + 1)],
                                             ps_t[:], AF.Copy, bias=0.0,
                                             scale=1.0).then_inc(S("l3b"), 1)
                            s.reg_mul(r1, tp, 512)
                            s.reg_add(r1, r1, 128 * qs)
                            s.reg_mul(r1, r1, E)
                            s.dma_start(out=AP(out, r1, [[E, 128], [1, 2048]]),
                                        in_=x2tm[:]).then_inc(S("st"), 16)
                            s.reg_mul(r1, tp, 64)
                            s.reg_add(r1, r1, 16 * qs + 16)
                            s.wait_ge(S("st"), r1)

            @blk.gpsimd
            def _(g):
                with g.register("rg36") as r1, g.register("rg37") as r2:
                    with FORI(g, 0, 4) as tp:
                        g.reg_mul(r1, tp, 90)
                        g.reg_add(r1, r1, 2)
                        g.reg_mul(r1, r1, 16)
                        g.wait_ge(S("ld3"), r1)
                        g.reg_mul(r1, tp, 112)
                        g.wait_ge(S("f2m"), r1)
                        for ecb in range(4):
                            for j in range(4):
                                g.reg_mul(r1, tp, 16)
                                g.reg_add(r1, r1, 4 * ecb + j + 1)
                                g.wait_ge(S("f2e"), r1)
                                g.tensor_tensor(
                                    AP(x2p, 512 * (4 * ecb + j),
                                       [[16 * 512, 128], [1, 512]]),
                                    dbuf[:, 512 * j:512 * (j + 1)],
                                    AP(XT1b, 512 * (4 * ecb + j),
                                       [[16 * 512, 128], [1, 512]]),
                                    op=ALU.add).then_inc(S("l3a"), 1)
                        g.reg_mul(r1, tp, 4)
                        g.reg_add(r1, r1, 1)
                        g.wait_ge(S("l3d"), r1)
                        g.tensor_tensor(m2r2[:], mrow2[:], mrow2[:], op=ALU.mult)
                        g.tensor_tensor(vrow2[:], qrow2[:], m2r2[:],
                                        op=ALU.subtract).then_inc(S("lnC"), 1)
                        g.reg_mul(r1, tp, 4)
                        g.reg_add(r1, r1, 4)
                        g.wait_ge(S("l3d"), r1)
                        with FORI(g, 0, 16) as ee:
                            g.reg_mul(r1, ee, 512)
                            goff3 = g.snap(r1, min_val=0, max_val=16 * 512 - 512)
                            g.tensor_tensor(ntf2[:],
                                            AP(x2p, goff3, [[16 * 512, 128], [1, 512]]),
                                            Mb2[:], op=ALU.subtract)
                            g.tensor_tensor(AP(x2p, goff3, [[16 * 512, 128], [1, 512]]),
                                            ntf2[:], Rb2[:], op=ALU.mult)
                        g.sem_inc(S("tv"), 1)

        with nc.Block() as blk:
            @blk.sync
            def _(sy):
                sy.wait_ge(S("st"), 16 * 16)

    return nc



def _host_masks(p):
    k = np.arange(128)[:, None]
    mA = np.zeros((128, 4 * 512), dtype=np.float32)
    mB = np.zeros((128, 4 * 512), dtype=np.float32)
    q = np.arange(512)[None, :]
    for o in range(4):
        mA[:, 512 * o:512 * (o + 1)] = (
            (128 * o + k) <= (512 * (q // 256) + 256 * p + (q % 256)))
        mB[:, 512 * o:512 * (o + 1)] = np.where(
            q < 256, 0.0,
            (128 * (4 + o) + k) <= (256 + 256 * p + q))
    return (np.ascontiguousarray(mA).astype(ml_dtypes.bfloat16),
            np.ascontiguousarray(mB).astype(ml_dtypes.bfloat16))


_NC_CACHE = [None]


def kernel(L0_attn_x, L0_ffn_lut, L0_ffn_gate_idx, L0_ffn_up_idx,
           L0_ffn_down_idx):
    x = np.asarray(L0_attn_x, dtype=np.float32)
    lut = np.asarray(L0_ffn_lut, dtype=np.float32).reshape(1, 16)
    gidx = np.asarray(L0_ffn_gate_idx, dtype=np.int32)
    uidx = np.asarray(L0_ffn_up_idx, dtype=np.int32)
    didx = np.asarray(L0_ffn_down_idx, dtype=np.int32)
    gu = np.ascontiguousarray(np.concatenate([gidx, uidx], axis=0))

    if _NC_CACHE[0] is None:
        _NC_CACHE[0] = build_program()
    nc = _NC_CACHE[0]

    in_maps = []
    for c in range(8):
        b, p = c // 2, c % 2
        xbat = np.ascontiguousarray(x[b])
        rows = np.concatenate(
            [np.arange(256 * (2 * u + p), 256 * (2 * u + p) + 256)
             for u in range(8)])
        xqrows = np.ascontiguousarray(xbat[rows])
        mA, mB = _host_masks(p)
        in_maps.append({"xb": xbat, "xq": xqrows, "lut": lut,
                        "gu_idx": gu, "d_idx": didx, "dmA": mA, "dmB": mB})

    res = run_bass_kernel_spmd(nc, in_maps, core_ids=list(range(8)))
    outp = np.zeros((B, L, E), dtype=np.float32)
    for c in range(8):
        b, p = c // 2, c % 2
        oc = np.asarray(res.results[c]["o"])
        for u in range(8):
            outp[b, 256 * (2 * u + p):256 * (2 * u + p) + 256] = \
                oc[256 * u:256 * (u + 1)]
    return outp


# version 07 (final)
# speedup vs baseline: 1.1190x; 1.1190x over previous
import contextlib
import os
import numpy as np
import ml_dtypes
import concourse.bass as bass
import concourse.mybir as mybir
from concourse.bass_utils import run_bass_kernel_spmd

F32 = mybir.dt.float32
BF16 = mybir.dt.bfloat16
I32 = mybir.dt.int32
AF = mybir.ActivationFunctionType
ALU = mybir.AluOpType
DR = mybir.MatmulPerfMode.DoubleRow

B, L, E, H, D = 4, 4096, 2048, 16, 128
INTER = 5504
INTP = 5632
INTP2 = 6144
NIC = 43
Q = 2048
SCALE = 1.0 / float(np.sqrt(D))
SLW = 1376
NGU = 128



import contextlib as _ctxlib
from concourse.ordered_engine_set import OrderedEngineSet as _OES


@_ctxlib.contextmanager
def FORI(eng, start, end, step=1):
    b = eng.bass
    engines = _OES([eng.engine])
    loop_id = b.next_id()
    loop_name = f"kfori_{loop_id}"
    loop_start = f"{loop_name}_loop"
    loop_end = f"{loop_name}_end"
    registers = b.alloc_registers(f"{loop_name}_i", engines=engines)
    b.regs_mov(registers, start)
    b.br(loop_start, engines=engines)
    from concourse.bass import sinput_min_value, sinput_max_value
    min_val = sinput_min_value(start)
    max_val = sinput_max_value(end)
    if max_val is not None:
        max_val -= 1
    with b.body(loop_start, valid_engines=engines):
        yield b.snap(registers, min_val=min_val, max_val=max_val)
        b.regs_alu(registers, registers, step, op=mybir.AluOpType.add)
        b.br_lt(registers, end, on_true=loop_start, on_false=loop_end,
                engines=engines)
    b.switch_bb(loop_end)
    for h_ in registers:
        eng.free_register(h_)


def AP(t, off, pattern):
    return bass.AP(t, off, pattern)


def build_program():
    nc = bass.Bass("TRN2", target_bir_lowering=False)

    xb = nc.dram_tensor("xb", [L, E], F32, kind="ExternalInput")
    xq = nc.dram_tensor("xq", [Q, E], F32, kind="ExternalInput")
    lut_in = nc.dram_tensor("lut", [1, 16], F32, kind="ExternalInput")
    gu_idx = nc.dram_tensor("gu_idx", [2 * INTER, E], I32, kind="ExternalInput")
    d_idx = nc.dram_tensor("d_idx", [E, INTER], I32, kind="ExternalInput")
    dmA = nc.dram_tensor("dmA", [128, 4 * 512], BF16, kind="ExternalInput")
    dmB = nc.dram_tensor("dmB", [128, 4 * 512], BF16, kind="ExternalInput")
    out = nc.dram_tensor("o", [Q, E], F32, kind="ExternalOutput")

    xbf = nc.dram_tensor("xbf", [L, E], BF16)
    xqbf = nc.dram_tensor("xqbf", [Q, E], BF16)
    xbfT = nc.dram_tensor("xbfT", [E, L], BF16)
    gub = nc.dram_tensor("gub", [2 * INTER, E], BF16)
    dbf = nc.dram_tensor("dbf", [E, INTER], BF16)
    wgu = nc.dram_tensor("wgu", [2 * E, INTP2], BF16)
    wdT = nc.dram_tensor("wdT", [INTP2, E], BF16)
    x1d = nc.dram_tensor("x1d", [E, Q], BF16)
    hd = nc.dram_tensor("hd", [48 * 128, Q], BF16)

    ctx = contextlib.ExitStack()
    EE = ctx.enter_context
    with ctx:
        sem = {}
        for n in ["cast", "pre", "cst", "lutld", "cxb", "cxq", "cgu", "cdb", "cpd",
                  "kw", "sc", "ex", "ve", "mk", "pv", "rc", "rb", "ye", "tv",
                  "idx", "bld", "wout",
                  "lnA", "lnB", "lnC", "l2x", "tev", "lnD",
                  "ld2", "ld2b", "f1m", "f1a", "f1b",
                  "ld3", "ld3x", "f2m", "f2e", "l3a", "l3b", "l3c", "l3d", "tv2",
                  "st"]:
            sem[n] = EE(nc.semaphore(n))

        def S(n):
            return sem[n]

        lutb = EE(nc.sbuf_tensor("lutb", [128, 16], F32))
        onesb = EE(nc.sbuf_tensor("onesb", [128, 1], BF16))
        onesb2 = EE(nc.sbuf_tensor("onesb2", [128, 2], BF16))
        onesbf = EE(nc.sbuf_tensor("onesbf", [1, 128], F32))
        onesff = EE(nc.sbuf_tensor("onesff", [128, 1], F32))
        zerot = EE(nc.sbuf_tensor("zerot", [128, 128], BF16))
        ident = EE(nc.sbuf_tensor("ident", [128, 128], BF16))
        identf = EE(nc.sbuf_tensor("identf", [128, 128], F32))
        zpad = EE(nc.sbuf_tensor("zpad", [128, 2048], BF16))
        mskA = EE(nc.sbuf_tensor("mskA", [128, 4 * 512], BF16))
        mskB = EE(nc.sbuf_tensor("mskB", [128, 4 * 512], BF16))
        iotai = EE(nc.sbuf_tensor("iotai", [128, 128], I32))
        eps1 = EE(nc.sbuf_tensor("eps1", [1, 1], F32))
        early = contextlib.ExitStack()
        XR = early.enter_context(nc.sbuf_tensor("XR", [128, H * Q], BF16))
        idxT = early.enter_context(nc.sbuf_tensor("idxT", [128, 2 * 2048], BF16))
        wterm = early.enter_context(nc.sbuf_tensor("wterm", [128, 2048], BF16))
        wbt = early.enter_context(nc.sbuf_tensor("wbt", [128, 2 * 2048], BF16))

        with (
            nc.sbuf_tensor("tpose", [128, L], BF16) as tpose,
            nc.psum_tensor("ps_lut", [128, 16], F32) as ps_lut,
            nc.Block() as blk,
        ):
            @blk.gpsimd
            def _(g):
                for i in range(4):
                    g.dma_start(out=xbf[i * 1024:(i + 1) * 1024, :],
                                in_=xb[i * 1024:(i + 1) * 1024, :]).then_inc(S("cxb"), 16)
                for i in range(2):
                    g.dma_start(out=xqbf[i * 1024:(i + 1) * 1024, :],
                                in_=xq[i * 1024:(i + 1) * 1024, :]).then_inc(S("cxq"), 16)
                for i in range(4):
                    g.dma_start(out=gub[i * 2752:(i + 1) * 2752, :],
                                in_=gu_idx[i * 2752:(i + 1) * 2752, :]).then_inc(S("cgu"), 16)
                for i in range(2):
                    g.dma_start(out=dbf[i * 1024:(i + 1) * 1024, :],
                                in_=d_idx[i * 1024:(i + 1) * 1024, :]).then_inc(S("cdb"), 16)
                g.iota(iotai[:], pattern=[[1, 128]], base=0,
                       channel_multiplier=-1).then_inc(S("cst"), 1)
                g.wait_ge(S("cst"), 2)
                for r in range(32):
                    g.dma_start(out=AP(wgu, r * 128 * INTP2 + INTER,
                                       [[INTP2, 128], [1, INTP2 - INTER]]),
                                in_=zpad[:, 0:INTP2 - INTER]
                                ).then_inc(S("cpd"), 16)
                for c in range(5):
                    g.dma_start(out=AP(wdT, (INTER + 128 * c) * E,
                                       [[E, 128], [1, 2048]]),
                                in_=zpad[:]).then_inc(S("cpd"), 16)
                for c in range(4):
                    g.dma_start(out=AP(hd, (INTP + 128 * c) * Q,
                                       [[Q, 128], [1, 2048]]),
                                in_=zpad[:]).then_inc(S("cpd"), 16)

            @blk.sync
            def _(sy):
                sy.dma_start(out=lutb[0:1, :], in_=lut_in[:]).then_inc(S("lutld"), 16)
                sy.dma_start(out=mskA[:], in_=dmA[:]).then_inc(S("pre"), 16)
                sy.dma_start(out=mskB[:], in_=dmB[:]).then_inc(S("pre"), 16)
                sy.wait_ge(S("cxb"), 4 * 16)
                pre = 32
                for e in range(16):
                    sy.wait_ge(S("pre"), pre)
                    sy.dma_start(out=tpose[:], in_=xbf[:, 128 * e:128 * (e + 1)],
                                 transpose=True).then_inc(S("pre"), 16)
                    pre += 16
                    sy.wait_ge(S("pre"), pre)
                    sy.dma_start(out=xbfT[128 * e:128 * (e + 1), :],
                                 in_=tpose[:]).then_inc(S("pre"), 16)
                    pre += 16
                sy.wait_ge(S("cxq"), 2 * 16)
                for h in range(16):
                    sy.dma_start(out=XR[:, Q * h:Q * (h + 1)],
                                 in_=xqbf[:, 128 * h:128 * (h + 1)],
                                 transpose=True).then_inc(S("pre"), 16)

            @blk.vector
            def _(v):
                v.memset(onesb[:], 1.0)
                v.memset(onesb2[:], 1.0)
                v.memset(onesbf[:], 1.0)
                v.memset(onesff[:], 1.0)
                v.memset(zerot[:], 0.0)
                v.memset(zpad[:], 0.0)
                v.memset(eps1[:], 1e-5)
                v.wait_ge(S("cst"), 1)
                v.tensor_scalar(ident[:], iotai[:], 0, 1.0,
                                op0=ALU.is_equal, op1=ALU.mult)
                v.tensor_scalar(identf[:], iotai[:], 0, 1.0,
                                op0=ALU.is_equal, op1=ALU.mult).then_inc(S("cst"), 1)

            @blk.tensor
            def _(t):
                t.wait_ge(S("lutld"), 16)
                t.wait_ge(S("cst"), 2)
                t.matmul(ps_lut[:], onesbf[:], lutb[0:1, :],
                         start=True, stop=True).then_inc(S("cst"), 1)

            @blk.vector
            def _(v):
                v.wait_ge(S("cst"), 3)
                v.tensor_copy(lutb[:], ps_lut[:]).then_inc(S("cst"), 1)

        PRE_DONE = 800

        with (
            nc.sbuf_tensor("kw", [128, 512], BF16) as kw,
            nc.sbuf_tensor("vw", [128, 512], BF16) as vw,
            nc.sbuf_tensor("PT", [128, 4 * 512], BF16) as PT,
            nc.sbuf_tensor("rsb", [1, 512], F32) as rsb,
            nc.sbuf_tensor("rt1", [1, 512], F32) as rt1,
            nc.sbuf_tensor("rbs", [128, 2 * 512], F32) as rbs,
            nc.sbuf_tensor("ybuf", [128, 2 * 512], F32) as ybuf,
            nc.sbuf_tensor("tvb", [128, 512], F32) as tvb,
            nc.psum_tensor("ps_s0", [128, 512], F32) as ps_s0,
            nc.psum_tensor("ps_s1", [128, 512], F32) as ps_s1,
            nc.psum_tensor("ps_s2", [128, 512], F32) as ps_s2,
            nc.psum_tensor("ps_s3", [128, 512], F32) as ps_s3,
            nc.psum_tensor("ps_y", [128, 512], F32) as ps_y,
            nc.psum_tensor("ps_sum", [1, 512], F32) as ps_sum,
            nc.psum_tensor("ps_va", [128, 128], BF16) as ps_va,
            nc.psum_tensor("ps_vb", [128, 128], BF16) as ps_vb,
            nc.Block() as blk,
        ):
            ps_s = [ps_s0, ps_s1, ps_s2, ps_s3]

            @blk.scalar
            def _(s):
                s.wait_ge(S("pre"), PRE_DONE)
                s.wait_ge(S("cst"), 4)
                with s.register("rg1") as wc, s.register("rg2") as r1, \
                        s.register("rg3") as r2:
                    s.reg_mov(wc, 0)
                    with FORI(s, 0, 16) as h:
                        for v in range(4):
                            nwin = 2 * v + 2

                            def act_window(wv):
                                s.reg_mul(r1, wc, 2)
                                s.wait_ge(S("sc"), r1)
                                if isinstance(wv, int):
                                    s.reg_mov(r2, 512 * wv)
                                else:
                                    s.reg_mul(r2, wv, 512)
                                s.reg_mul(r1, h, 128 * L)
                                s.reg_add(r1, r1, r2)
                                s.dma_start(out=kw[:],
                                            in_=AP(xbfT, r1, [[L, 128], [1, 512]])
                                            ).then_inc(S("kw"), 16)
                                s.reg_mul(r1, wc, 2)
                                s.reg_add(r1, r1, 1)
                                s.wait_ge(S("sc"), r1)
                                s.activation(vw[:, 0:128], ps_va[:], AF.Copy,
                                             bias=0.0, scale=1.0).then_inc(S("ve"), 1)
                                s.activation(vw[:, 128:256], ps_vb[:], AF.Copy,
                                             bias=0.0, scale=1.0).then_inc(S("ve"), 1)
                                s.reg_mul(r1, wc, 2)
                                s.reg_add(r1, r1, 2)
                                s.wait_ge(S("sc"), r1)
                                s.wait_ge(S("pv"), wc)
                                s.activation(vw[:, 256:384], ps_va[:], AF.Copy,
                                             bias=0.0, scale=1.0).then_inc(S("ve"), 1)
                                s.activation(vw[:, 384:512], ps_vb[:], AF.Copy,
                                             bias=0.0, scale=1.0).then_inc(S("ve"), 1)
                                for j in range(4):
                                    a = s.activation(PT[:, 512 * j:512 * (j + 1)],
                                                     ps_s[j][:], AF.Exp,
                                                     bias=0.0, scale=SCALE)
                                    if j == 3:
                                        a.then_inc(S("ex"), 1)
                                s.reg_add(wc, wc, 1)

                            with FORI(s, 0, nwin) as w:
                                act_window(w)
                            s.wait_ge(S("pv"), wc)
                            s.activation(rt1[:], ps_sum[:], AF.Ln, bias=0.0,
                                         scale=1.0)
                            s.activation(rsb[:], rt1[:], AF.Exp, bias=0.0,
                                         scale=-1.0).then_inc(S("rc"), 1)
                            s.reg_mul(r1, h, 4)
                            s.reg_add(r1, r1, v + 1)
                            s.wait_ge(S("rb"), r1)
                            s.reg_mul(r1, h, 4)
                            s.reg_add(r1, r1, v)
                            s.reg_alu(r1, r1, 1, ALU.subtract)
                            s.reg_alu(r1, r1, 0, ALU.max)
                            s.wait_ge(S("tv"), r1)
                            pr = v % 2
                            s.activation(rbs[:, 512 * pr:512 * (pr + 1)], ps_s0[:],
                                         AF.Copy, bias=0.0, scale=1.0)
                            s.activation(ybuf[:, 512 * pr:512 * (pr + 1)], ps_y[:],
                                         AF.Copy, bias=0.0, scale=1.0).then_inc(S("ye"), 1)

            @blk.tensor
            def _(t):
                with t.register("rg4") as wc, t.register("rg5") as r1, \
                        t.register("rg6") as r2:
                    t.reg_mov(wc, 0)
                    with FORI(t, 0, 16) as h:
                        for v in range(4):
                            nwin = 2 * v + 2
                            t.reg_mul(r1, h, 4)
                            t.reg_add(r1, r1, v)
                            t.wait_ge(S("ye"), r1)
                            t.wait_ge(S("rc"), r1)
                            t.matmul(ps_y[:], zerot[:], mskA[:, 0:512],
                                     start=True, stop=False)
                            t.matmul(ps_sum[:], zerot[:, 0:1], mskA[:, 0:512],
                                     start=True, stop=False)

                            with FORI(t, 0, nwin) as w:
                                t.reg_mul(r1, wc, 16)
                                t.reg_add(r1, r1, 16)
                                t.wait_ge(S("kw"), r1)
                                t.wait_ge(S("ex"), wc)
                                t.reg_mul(r1, wc, 4)
                                t.wait_ge(S("ve"), r1)
                                t.transpose(ps_va[:], kw[:, 0:128], ident[:])
                                t.transpose(ps_vb[:], kw[:, 128:256], ident[:]
                                            ).then_inc(S("sc"), 1)
                                t.reg_mul(r2, h, Q)
                                t.reg_add(r2, r2, 512 * v)
                                xoff = t.snap(r2, min_val=0, max_val=H * Q - 512)
                                for j in range(4):
                                    t.matmul(ps_s[j][:],
                                             kw[:, 128 * j:128 * (j + 1)],
                                             AP(XR, xoff, [[H * Q, 128], [1, 512]]),
                                             start=True, stop=True)
                                t.reg_mul(r1, wc, 4)
                                t.reg_add(r1, r1, 2)
                                t.wait_ge(S("ve"), r1)
                                t.transpose(ps_va[:], kw[:, 256:384], ident[:])
                                t.transpose(ps_vb[:], kw[:, 384:512], ident[:]
                                            ).then_inc(S("sc"), 1)
                                t.reg_add(r1, wc, 1)
                                t.wait_ge(S("ex"), r1)
                                t.reg_mul(r1, h, 8)
                                t.reg_add(r1, r1, w)
                                t.reg_add(r1, r1, 1)
                                t.wait_ge(S("mk"), r1)
                                for j in range(4):
                                    t.matmul(ps_y[:],
                                             vw[:, 128 * j:128 * (j + 1)],
                                             PT[:, 512 * j:512 * (j + 1)],
                                             start=False, stop=False)
                                for j in range(4):
                                    mm = t.matmul(ps_sum[:], onesb[:],
                                                  PT[:, 512 * j:512 * (j + 1)],
                                                  start=False, stop=False)
                                mm.then_inc(S("pv"), 1)
                                t.reg_add(wc, wc, 1)
                            t.matmul(ps_y[:], zerot[:], mskA[:, 0:512],
                                     start=False, stop=True)
                            t.matmul(ps_sum[:], zerot[:, 0:1], mskA[:, 0:512],
                                     start=False, stop=True)
                            t.reg_mul(r1, h, 4)
                            t.reg_add(r1, r1, v + 1)
                            t.wait_ge(S("rc"), r1)
                            t.wait_ge(S("ex"), wc)
                            t.matmul(ps_s0[:], onesbf[:], rsb[:],
                                     start=True, stop=True).then_inc(S("rb"), 1)

            @blk.gpsimd
            def _(g):
                with g.register("rg7") as r1, g.register("rg8") as r2:
                    with FORI(g, 0, 16) as h:
                        for v in range(4):
                            nwin = 2 * v + 2
                            base = v * v + v
                            for ki, msk in ((0, mskA), (1, mskB)):
                                wce = base + nwin - 2 + ki
                                g.reg_mul(r1, h, 20)
                                g.reg_add(r1, r1, wce + 1)
                                g.wait_ge(S("ex"), r1)
                                for j in range(4):
                                    tt = g.tensor_tensor(PT[:, 512 * j:512 * (j + 1)],
                                                         PT[:, 512 * j:512 * (j + 1)],
                                                         msk[:, 512 * j:512 * (j + 1)],
                                                         op=ALU.mult)
                                tt.then_inc(S("mk"), 1)
                            g.reg_mul(r1, h, 4)
                            g.reg_add(r1, r1, v + 1)
                            g.wait_ge(S("ye"), r1)
                            pr = v % 2
                            g.tensor_tensor(tvb[:], ybuf[:, 512 * pr:512 * (pr + 1)],
                                            rbs[:, 512 * pr:512 * (pr + 1)],
                                            op=ALU.mult)
                            g.reg_mul(r2, h, Q)
                            g.reg_add(r2, r2, 512 * v)
                            goff = g.snap(r2, min_val=0, max_val=H * Q - 512)
                            g.tensor_tensor(AP(XR, goff, [[H * Q, 128], [1, 512]]),
                                            AP(XR, goff, [[H * Q, 128], [1, 512]]),
                                            tvb[:], op=ALU.add).then_inc(S("tv"), 1)

            @blk.vector
            def _(v):
                v.wait_ge(S("cst"), 4)
                with v.register("rg9") as r1:
                    with FORI(v, 0, NGU // 2) as i:
                        for par in range(2):
                            v.reg_mul(r1, i, 32)
                            v.reg_add(r1, r1, 16 * par + 16)
                            v.wait_ge(S("idx"), r1)
                            v.reg_mul(r1, i, 32)
                            if par == 0:
                                v.reg_alu(r1, r1, 16, ALU.subtract)
                                v.reg_alu(r1, r1, 0, ALU.max)
                            v.wait_ge(S("wout"), r1)
                            sl = 2048 * par
                            v.memset(wbt[:, sl:sl + SLW], 0.0)
                            for k in range(16):
                                v.tensor_scalar(wterm[:, 0:SLW],
                                                idxT[:, sl:sl + SLW], float(k),
                                                lutb[:, k:k + 1],
                                                op0=ALU.is_equal, op1=ALU.mult)
                                tt = v.tensor_tensor(wbt[:, sl:sl + SLW],
                                                     wbt[:, sl:sl + SLW],
                                                     wterm[:, 0:SLW], op=ALU.add)
                            tt.then_inc(S("bld"), 1)

            @blk.sync
            def _(sy):
                sy.wait_ge(S("cgu"), 4 * 16)
                with sy.register("rg10") as r1, sy.register("rg11") as r2, \
                        sy.register("rg12") as rT:
                    with FORI(sy, 0, NGU // 2) as i:
                        for par in range(2):
                            sy.reg_mul(rT, i, 2)
                            sy.reg_add(rT, rT, par)
                            sy.reg_alu(r1, rT, 1, ALU.subtract)
                            sy.reg_alu(r1, r1, 0, ALU.max)
                            sy.wait_ge(S("bld"), r1)
                            sy.reg_div(r1, rT, 32)
                            sy.reg_mul(r1, r1, SLW)
                            sy.reg_mod(r2, rT, 32)
                            sy.reg_div(r2, r2, 16)
                            sy.reg_mul(r2, r2, INTER)
                            sy.reg_add(r1, r1, r2)
                            sy.reg_mul(r1, r1, E)
                            sy.reg_mod(r2, rT, 16)
                            sy.reg_mul(r2, r2, 128)
                            sy.reg_add(r1, r1, r2)
                            sy.dma_start(out=idxT[:, 2048 * par:2048 * par + SLW],
                                         in_=AP(gub, r1, [[E, SLW], [1, 128]]),
                                         transpose=True).then_inc(S("idx"), 16)
                            sy.reg_add(r1, rT, 1)
                            sy.wait_ge(S("bld"), r1)
                            sy.reg_mod(r1, rT, 32)
                            sy.reg_mul(r1, r1, 128 * INTP2)
                            sy.reg_div(r2, rT, 32)
                            sy.reg_mul(r2, r2, SLW)
                            sy.reg_add(r1, r1, r2)
                            sy.dma_start(out=AP(wgu, r1, [[INTP2, 128], [1, SLW]]),
                                         in_=wbt[:, 2048 * par:2048 * par + SLW]
                                         ).then_inc(S("wout"), 16)

        with (
            nc.sbuf_tensor("sqf", [128, 2 * 512], F32) as sqf,
            nc.sbuf_tensor("mrow", [1, 512], F32) as mrow,
            nc.sbuf_tensor("qrow", [1, 512], F32) as qrow,
            nc.sbuf_tensor("m2row", [1, 512], F32) as m2row,
            nc.sbuf_tensor("vrow", [1, 512], F32) as vrow,
            nc.sbuf_tensor("rrow", [1, 512], F32) as rrow,
            nc.sbuf_tensor("Mb", [128, 512], F32) as Mb,
            nc.sbuf_tensor("Rb", [128, 512], F32) as Rb,
            nc.sbuf_tensor("ntf", [128, 512], F32) as ntf,
            nc.sbuf_tensor("xt1t", [128, 512], BF16) as xt1t,
            nc.psum_tensor("ps_st", [1, 512], F32) as ps_st,
            nc.psum_tensor("ps_sq", [1, 512], F32) as ps_sq,
            nc.psum_tensor("ps_bc", [128, 512], F32) as ps_bc,
            nc.Block() as blk,
        ):
            @blk.scalar
            def _(s):
                s.wait_ge(S("tv"), 64)
                with s.register("sA") as r1, s.register("sB") as r2, \
                        s.register("sC") as r3:
                    with FORI(s, 0, 4) as qb:
                        with FORI(s, 0, 16) as hh:
                            s.reg_alu(r1, hh, 1, ALU.subtract)
                            s.reg_alu(r1, r1, 0, ALU.max)
                            s.reg_mul(r3, qb, 17)
                            s.reg_add(r1, r1, r3)
                            s.wait_ge(S("lnB"), r1)
                            s.reg_mod(r2, hh, 2)
                            s.reg_mul(r2, r2, 512)
                            soff2 = s.snap(r2, min_val=0, max_val=512)
                            s.reg_mul(r1, hh, Q)
                            s.reg_mul(r3, qb, 512)
                            s.reg_add(r1, r1, r3)
                            soff1 = s.snap(r1, min_val=0, max_val=H * Q - 512)
                            s.activation(AP(sqf, soff2, [[1024, 128], [1, 512]]),
                                         AP(XR, soff1, [[H * Q, 128], [1, 512]]),
                                         AF.Square, bias=0.0, scale=1.0
                                         ).then_inc(S("lnA"), 1)
                        s.reg_mul(r1, qb, 17)
                        s.reg_add(r1, r1, 17)
                        s.wait_ge(S("lnB"), r1)
                        s.activation(mrow[:], ps_st[:], AF.Copy, bias=0.0,
                                     scale=1.0 / E)
                        s.activation(qrow[:], ps_sq[:], AF.Copy, bias=0.0,
                                     scale=1.0 / E).then_inc(S("l2x"), 1)
                        s.reg_add(r1, qb, 1)
                        s.wait_ge(S("lnC"), r1)
                        s.activation(vrow[:], vrow[:], AF.Sqrt, bias=eps1[:],
                                     scale=1.0)
                        s.activation(vrow[:], vrow[:], AF.Ln, bias=0.0, scale=1.0)
                        s.activation(rrow[:], vrow[:], AF.Exp, bias=0.0,
                                     scale=-1.0).then_inc(S("l2x"), 1)
                        s.reg_mul(r1, qb, 2)
                        s.reg_add(r1, r1, 1)
                        s.wait_ge(S("tev"), r1)
                        s.activation(Mb[:], ps_bc[:], AF.Copy, bias=0.0,
                                     scale=1.0).then_inc(S("l2x"), 1)
                        s.reg_mul(r1, qb, 2)
                        s.reg_add(r1, r1, 2)
                        s.wait_ge(S("tev"), r1)
                        s.activation(Rb[:], ps_bc[:], AF.Copy, bias=0.0,
                                     scale=1.0).then_inc(S("l2x"), 1)

            @blk.tensor
            def _(t):
                with t.register("tA") as r1, t.register("tB") as r2, \
                        t.register("tC") as r3:
                    t.wait_ge(S("tv"), 64)
                    with FORI(t, 0, 4) as qb:
                        t.reg_mul(r1, qb, 4)
                        t.reg_alu(r1, r1, 3, ALU.subtract)
                        t.reg_alu(r1, r1, 0, ALU.max)
                        t.wait_ge(S("l2x"), r1)
                        t.matmul(ps_st[:], zerot[:, 0:1], mskA[:, 0:512],
                                 start=True, stop=False)
                        t.matmul(ps_sq[:], zerot[:, 0:1], mskA[:, 0:512],
                                 start=True, stop=False)
                        with FORI(t, 0, 16) as hh:
                            t.reg_mul(r1, hh, Q)
                            t.reg_mul(r3, qb, 512)
                            t.reg_add(r1, r1, r3)
                            toff1 = t.snap(r1, min_val=0, max_val=H * Q - 512)
                            t.matmul(ps_st[:], onesb[:],
                                     AP(XR, toff1, [[H * Q, 128], [1, 512]]),
                                     start=False, stop=False)
                            t.reg_mul(r3, qb, 16)
                            t.reg_add(r1, hh, 1)
                            t.reg_alu(r1, r1, r3, ALU.add)
                            t.wait_ge(S("lnA"), r1)
                            t.reg_mod(r2, hh, 2)
                            t.reg_mul(r2, r2, 512)
                            toff2 = t.snap(r2, min_val=0, max_val=512)
                            t.matmul(ps_sq[:], onesff[:],
                                     AP(sqf, toff2, [[1024, 128], [1, 512]]),
                                     start=False, stop=False).then_inc(S("lnB"), 1)
                        t.matmul(ps_st[:], zerot[:, 0:1], mskA[:, 0:512],
                                 start=False, stop=True)
                        t.matmul(ps_sq[:], zerot[:, 0:1], mskA[:, 0:512],
                                 start=False, stop=True).then_inc(S("lnB"), 1)
                        t.reg_mul(r1, qb, 4)
                        t.reg_add(r1, r1, 1)
                        t.wait_ge(S("l2x"), r1)
                        t.matmul(ps_bc[:], onesbf[:], mrow[:],
                                 start=True, stop=True).then_inc(S("tev"), 1)
                        t.reg_mul(r1, qb, 4)
                        t.reg_add(r1, r1, 3)
                        t.wait_ge(S("l2x"), r1)
                        t.matmul(ps_bc[:], onesbf[:], rrow[:],
                                 start=True, stop=True).then_inc(S("tev"), 1)

            @blk.gpsimd
            def _(g):
                with g.register("gA") as r1, g.register("gB") as r2:
                    with FORI(g, 0, 4) as qb:
                        g.reg_mul(r1, qb, 4)
                        g.reg_add(r1, r1, 1)
                        g.wait_ge(S("l2x"), r1)
                        g.tensor_tensor(m2row[:], mrow[:], mrow[:], op=ALU.mult)
                        g.tensor_tensor(vrow[:], qrow[:], m2row[:],
                                        op=ALU.subtract).then_inc(S("lnC"), 1)
                        g.reg_mul(r1, qb, 4)
                        g.reg_add(r1, r1, 4)
                        g.wait_ge(S("l2x"), r1)
                        with FORI(g, 0, 16) as hh:
                            g.reg_mul(r1, hh, Q)
                            g.reg_mul(r2, qb, 512)
                            g.reg_add(r1, r1, r2)
                            goff1 = g.snap(r1, min_val=0, max_val=H * Q - 512)
                            g.tensor_tensor(ntf[:],
                                            AP(XR, goff1, [[H * Q, 128], [1, 512]]),
                                            Mb[:], op=ALU.subtract)
                            g.tensor_tensor(xt1t[:], ntf[:], Rb[:], op=ALU.mult)
                            g.reg_mul(r1, qb, 128 * 16 * 512)
                            g.reg_mul(r2, hh, 512)
                            g.reg_add(r1, r1, r2)
                            g.dma_start(out=AP(x1d, r1, [[16 * 512, 128], [1, 512]]),
                                        in_=xt1t[:]).then_inc(S("lnD"), 16)
                            g.reg_mul(r1, hh, 16)
                            g.reg_mul(r2, qb, 256)
                            g.reg_add(r1, r1, r2)
                            g.reg_add(r1, r1, 16)
                            g.wait_ge(S("lnD"), r1)

            @blk.vector
            def _(v):
                with v.register("rg18") as r1:
                    with FORI(v, 0, 44) as T2:
                        v.reg_mul(r1, T2, 16)
                        v.reg_add(r1, r1, NGU * 16 + 16)
                        v.wait_ge(S("idx"), r1)
                        v.reg_mul(r1, T2, 16)
                        v.reg_add(r1, r1, NGU * 16)
                        v.wait_ge(S("wout"), r1)
                        sl = 0
                        v.memset(wbt[:, sl:sl + 2048], 0.0)
                        for k in range(16):
                            v.tensor_scalar(wterm[:, 0:2048],
                                            idxT[:, sl:sl + 2048], float(k),
                                            lutb[:, k:k + 1],
                                            op0=ALU.is_equal, op1=ALU.mult)
                            tt = v.tensor_tensor(wbt[:, sl:sl + 2048],
                                                 wbt[:, sl:sl + 2048],
                                                 wterm[:, 0:2048], op=ALU.add)
                        tt.then_inc(S("bld"), 1)

            @blk.scalar
            def _(sy):
                sy.wait_ge(S("cdb"), 2 * 16)
                with sy.register("rg19") as r1, sy.register("rg20") as r2:
                    with FORI(sy, 0, 44) as T2:
                        sy.reg_alu(r2, T2, 42, ALU.min)
                        sy.reg_add(r1, T2, NGU)
                        sy.wait_ge(S("bld"), r1)
                        sy.reg_mul(r1, r2, 128)
                        sy.dma_start(out=idxT[:, 0:2048],
                                     in_=AP(dbf, r1, [[INTER, 2048], [1, 128]]),
                                     transpose=True).then_inc(S("idx"), 16)
                        sy.reg_add(r1, T2, NGU + 1)
                        sy.wait_ge(S("bld"), r1)
                        sy.reg_mul(r1, r2, 128 * E)
                        sy.dma_start(out=AP(wdT, r1, [[E, 128], [1, 2048]]),
                                     in_=wbt[:, 0:2048]).then_inc(S("wout"), 16)

        early.close()

        with (
            nc.sbuf_tensor("XT1", [128, 16 * 512], BF16) as XT1,
            nc.sbuf_tensor("WgA", [128, 16384], BF16) as WgA,
            nc.sbuf_tensor("WuA", [128, 16384], BF16) as WuA,
            nc.sbuf_tensor("WgB", [128, 16384], BF16) as WgB,
            nc.sbuf_tensor("WuB", [128, 16384], BF16) as WuB,
            nc.sbuf_tensor("gsil", [128, 2 * 512], BF16) as gsil,
            nc.sbuf_tensor("ubuf", [128, 2 * 512], F32) as ubuf,
            nc.sbuf_tensor("hq", [128, 512], BF16) as hq,
            nc.psum_tensor("ps_g0", [128, 512], F32) as ps_g0,
            nc.psum_tensor("ps_u0", [128, 512], F32) as ps_u0,
            nc.psum_tensor("ps_g1", [128, 512], F32) as ps_g1,
            nc.psum_tensor("ps_u1", [128, 512], F32) as ps_u1,
            nc.Block() as blk,
        ):
            ps_g = [ps_g0, ps_g1]
            ps_u = [ps_u0, ps_u1]
            def _wpair3(sy, r1, r2, tp, pb, add1, gt, ut):
                sy.reg_add(r1, pb, add1 + 1)
                sy.reg_mul(r1, r1, 1024)
                sy.reg_add(r1, r1, SLW - 1)
                sy.reg_div(r1, r1, SLW)
                sy.reg_alu(r1, r1, 4, ALU.min)
                sy.reg_mul(r1, r1, 32 * 16)
                sy.wait_ge(S("wout"), r1)
                sy.reg_add(r1, pb, add1)
                sy.reg_mul(r1, r1, 8)
                sy.reg_alu(r1, r1, 8, ALU.subtract)
                sy.reg_alu(r1, r1, 0, ALU.max)
                sy.reg_mul(r2, tp, 44)
                sy.reg_add(r1, r1, r2)
                sy.wait_ge(S("f1m"), r1)
                for mat, gu in ((gt, 0), (ut, 1)):
                    for hf in range(2):
                        sy.reg_mul(r1, pb, 1024)
                        sy.reg_add(r1, r1, add1 * 1024 + gu * E * INTP2
                                   + hf * 8 * 128 * INTP2)
                        sy.dma_start(
                            out=mat[:, 8192 * hf:8192 * (hf + 1)],
                            in_=AP(wgu, r1,
                                   [[INTP2, 128], [128 * INTP2, 8], [1, 1024]])
                            ).then_inc(S("ld2"), 16)

            @blk.sync
            def _(sy):
                sy.wait_ge(S("lnD"), 16 * 64)
                sy.wait_ge(S("cpd"), 41 * 16)
                with sy.register("rg21") as r1, sy.register("rg22") as r2:
                    with FORI(sy, 0, 4) as tp:
                        sy.reg_mul(r1, tp, 44)
                        sy.wait_ge(S("f1m"), r1)
                        sy.reg_mul(r1, tp, 128 * 16 * 512)
                        sy.dma_start(out=XT1[:],
                                     in_=AP(x1d, r1, [[16 * 512, 128], [1, 16 * 512]])
                                     ).then_inc(S("ld2"), 16)
                        with FORI(sy, 0, 6, step=2) as pb:
                            _wpair3(sy, r1, r2, tp, pb, 0, WgA, WuA)
                            _wpair3(sy, r1, r2, tp, pb, 1, WgB, WuB)

            @blk.tensor
            def _(t):
                with t.register("rg23") as r1, t.register("rg24") as rc2, \
                        t.register("rg25") as r2:
                    t.reg_mov(rc2, 0)

                    def _gicg3(gt, ut, half):
                        for ic in range(4):
                            par = ic % 2
                            t.reg_alu(r1, rc2, 1, ALU.subtract)
                            t.reg_alu(r1, r1, 0, ALU.max)
                            t.reg_mul(r1, r1, 16)
                            t.wait_ge(S("f1b"), r1)
                            for hc in range(16):
                                t.matmul(ps_g[par][:],
                                         gt[:, 1024 * hc + 512 * half + 128 * ic:
                                            1024 * hc + 512 * half + 128 * (ic + 1)],
                                         XT1[:, 512 * hc:512 * (hc + 1)],
                                         start=(hc == 0), stop=(hc == 15))
                            for hc in range(16):
                                mm = t.matmul(ps_u[par][:],
                                              ut[:, 1024 * hc + 512 * half + 128 * ic:
                                                 1024 * hc + 512 * half + 128 * (ic + 1)],
                                              XT1[:, 512 * hc:512 * (hc + 1)],
                                              start=(hc == 0), stop=(hc == 15))
                            mm.then_inc(S("f1m"), 1)
                            t.reg_add(rc2, rc2, 1)

                    def _ld2wait(tp, pb, cnt):
                        t.reg_mul(r1, tp, 25)
                        if isinstance(pb, int):
                            t.reg_add(r1, r1, 1 + 4 * pb + cnt)
                        else:
                            t.reg_mul(r2, pb, 4)
                            t.reg_add(r1, r1, r2)
                            t.reg_add(r1, r1, 1 + cnt)
                        t.reg_mul(r1, r1, 16)
                        t.wait_ge(S("ld2"), r1)

                    with FORI(t, 0, 4) as tp:
                        with FORI(t, 0, 4, step=2) as pb:
                            _ld2wait(tp, pb, 4)
                            _gicg3(WgA, WuA, 0)
                            _gicg3(WgA, WuA, 1)
                            _ld2wait(tp, pb, 8)
                            _gicg3(WgB, WuB, 0)
                            _gicg3(WgB, WuB, 1)
                        _ld2wait(tp, 4, 4)
                        _gicg3(WgA, WuA, 0)
                        _gicg3(WgA, WuA, 1)
                        _ld2wait(tp, 4, 8)
                        _gicg3(WgB, WuB, 0)

            @blk.scalar
            def _(s):
                with s.register("rg26") as r1:
                    with FORI(s, 0, 88) as i2:
                        for par in range(2):
                            s.reg_mul(r1, i2, 2)
                            s.reg_add(r1, r1, par + 1)
                            s.wait_ge(S("f1m"), r1)
                            s.reg_mul(r1, i2, 2)
                            s.reg_add(r1, r1, par)
                            s.reg_alu(r1, r1, 1, ALU.subtract)
                            s.reg_alu(r1, r1, 0, ALU.max)
                            s.reg_mul(r1, r1, 16)
                            s.wait_ge(S("f1b"), r1)
                            s.activation(gsil[:, 512 * par:512 * (par + 1)],
                                         ps_g[par][:], AF.Silu, bias=0.0,
                                         scale=1.0)
                            s.activation(ubuf[:, 512 * par:512 * (par + 1)],
                                         ps_u[par][:], AF.Copy, bias=0.0,
                                         scale=1.0).then_inc(S("f1a"), 1)

            @blk.gpsimd
            def _(g):
                with g.register("rg27") as r1, g.register("rg28") as r2:
                    with FORI(g, 0, 88) as i2:
                        for par in range(2):
                            g.reg_mul(r1, i2, 2)
                            g.reg_add(r1, r1, par + 1)
                            g.wait_ge(S("f1a"), r1)
                            g.tensor_tensor(hq[:],
                                            gsil[:, 512 * par:512 * (par + 1)],
                                            ubuf[:, 512 * par:512 * (par + 1)],
                                            op=ALU.mult)
                            g.reg_mul(r1, i2, 2)
                            g.reg_add(r1, r1, par)
                            g.reg_div(r2, r1, 44)
                            g.reg_mod(r1, r1, 44)
                            g.reg_mul(r1, r1, 128 * Q)
                            g.reg_mul(r2, r2, 512)
                            g.reg_add(r1, r1, r2)
                            g.dma_start(out=AP(hd, r1, [[Q, 128], [1, 512]]),
                                        in_=hq[:]).then_inc(S("f1b"), 16)
                            g.reg_mul(r1, i2, 2)
                            g.reg_add(r1, r1, par + 1)
                            g.reg_mul(r1, r1, 16)
                            g.wait_ge(S("f1b"), r1)

        with contextlib.ExitStack() as ph4ctx:
            P4 = ph4ctx.enter_context
            XT1b = P4(nc.sbuf_tensor("XT1b", [128, 16 * 512], BF16))
            WdA = P4(nc.sbuf_tensor("WdA", [128, 8 * 512], BF16))
            hsA = P4(nc.sbuf_tensor("hsA", [128, 8 * 512], BF16))
            WdB = P4(nc.sbuf_tensor("WdB", [128, 8 * 512], BF16))
            hsB = P4(nc.sbuf_tensor("hsB", [128, 8 * 512], BF16))
            dbuf = P4(nc.sbuf_tensor("dbuf", [128, 4 * 512], F32))
            x2p = P4(nc.sbuf_tensor("x2p", [128, 16 * 512], F32))
            sq2 = P4(nc.sbuf_tensor("sq2", [128, 2 * 512], F32))
            mrow2 = P4(nc.sbuf_tensor("mrow2", [1, 512], F32))
            qrow2 = P4(nc.sbuf_tensor("qrow2", [1, 512], F32))
            m2r2 = P4(nc.sbuf_tensor("m2r2", [1, 512], F32))
            vrow2 = P4(nc.sbuf_tensor("vrow2", [1, 512], F32))
            rrow2 = P4(nc.sbuf_tensor("rrow2", [1, 512], F32))
            Mb2 = P4(nc.sbuf_tensor("Mb2", [128, 512], F32))
            Rb2 = P4(nc.sbuf_tensor("Rb2", [128, 512], F32))
            ntf2 = P4(nc.sbuf_tensor("ntf2", [128, 512], F32))
            x2tm = P4(nc.sbuf_tensor("x2tm", [128, 2048], F32))
            ps_d0 = P4(nc.psum_tensor("ps_d0", [128, 512], F32))
            ps_d1 = P4(nc.psum_tensor("ps_d1", [128, 512], F32))
            ps_d2 = P4(nc.psum_tensor("ps_d2", [128, 512], F32))
            ps_d3 = P4(nc.psum_tensor("ps_d3", [128, 512], F32))
            ps_t = P4(nc.psum_tensor("ps_t", [128, 128], F32))
            ps_s2r = P4(nc.psum_tensor("ps_s2r", [1, 512], F32))
            ps_q2r = P4(nc.psum_tensor("ps_q2r", [1, 512], F32))
            blk = P4(nc.Block())

            ps_d = [ps_d0, ps_d1, ps_d2, ps_d3]

            def _wpair4(sy, r1, r2, r3, tp, ecb, pb, add1, wd, hh):
                sy.reg_add(r1, pb, add1)
                sy.reg_mul(r1, r1, 8)
                sy.reg_add(r1, r1, 8)
                sy.reg_alu(r1, r1, 43, ALU.min)
                sy.reg_add(r1, r1, NGU)
                sy.reg_mul(r1, r1, 16)
                sy.wait_ge(S("wout"), r1)
                sy.reg_add(r1, pb, add1)
                sy.reg_mul(r1, r1, 2)
                sy.reg_alu(r1, r1, 2, ALU.subtract)
                sy.reg_alu(r1, r1, 0, ALU.max)
                sy.reg_mul(r2, tp, 112)
                sy.reg_add(r1, r1, r2)
                sy.reg_mul(r2, ecb, 12)
                sy.reg_add(r1, r1, r2)
                sy.wait_ge(S("f2m"), r1)
                sy.reg_add(r2, pb, add1)
                sy.reg_mul(r2, r2, 1024 * E)
                sy.reg_mul(r3, ecb, 512)
                sy.reg_add(r2, r2, r3)
                sy.dma_start(
                    out=wd[:],
                    in_=AP(wdT, r2, [[E, 128], [128 * E, 8], [1, 512]])
                    ).then_inc(S("ld3"), 16)
                sy.reg_add(r2, pb, add1)
                sy.reg_mul(r2, r2, 1024 * Q)
                sy.reg_mul(r3, tp, 512)
                sy.reg_add(r2, r2, r3)
                sy.dma_start(
                    out=hh[:],
                    in_=AP(hd, r2, [[Q, 128], [128 * Q, 8], [1, 512]])
                    ).then_inc(S("ld3"), 16)

            @blk.sync
            def _(sy):
                sy.wait_ge(S("f1b"), 176 * 16)
                sy.wait_ge(S("cpd"), 41 * 16)
                with sy.register("rg29") as r1, sy.register("rg30") as r2, \
                        sy.register("rg31") as r3:
                    with FORI(sy, 0, 4) as tp:
                        sy.reg_mul(r1, tp, 16)
                        sy.wait_ge(S("l3a"), r1)
                        sy.reg_mul(r1, tp, 128 * 16 * 512)
                        sy.dma_start(out=XT1b[:],
                                     in_=AP(x1d, r1, [[16 * 512, 128], [1, 16 * 512]])
                                     ).then_inc(S("ld3x"), 16)
                        with FORI(sy, 0, 4) as ecb:
                            with FORI(sy, 0, 6, step=2) as pb:
                                _wpair4(sy, r1, r2, r3, tp, ecb, pb, 0,
                                        WdA, hsA)
                                _wpair4(sy, r1, r2, r3, tp, ecb, pb, 1,
                                        WdB, hsB)

            @blk.tensor
            def _(t):
                with t.register("rg32") as r1, t.register("rg33") as r2:
                    with FORI(t, 0, 4) as tp:
                        with FORI(t, 0, 4) as ecb:
                            t.reg_mul(r1, tp, 4)
                            t.wait_ge(S("l3d"), r1)
                            t.reg_mul(r1, tp, 16)
                            t.reg_mul(r2, ecb, 4)
                            t.reg_add(r1, r1, r2)
                            t.wait_ge(S("f2e"), r1)
                            for j in range(4):
                                t.matmul(ps_d[j][:], zerot[:], mskA[:, 0:512],
                                         start=True, stop=False)
                            def _dicg4(wd, hh, half):
                                for sblk in range(4):
                                    c0 = 2048 * half + 512 * sblk
                                    for j in range(4):
                                        mm = t.matmul(
                                            ps_d[j][:],
                                            wd[:, c0 + 128 * j:
                                               c0 + 128 * (j + 1)],
                                            hh[:, c0:c0 + 512],
                                            start=False, stop=False)
                                mm.then_inc(S("f2m"), 1)

                            def _ld3wait(tp, ecb, pb, cnt):
                                t.reg_mul(r1, tp, 48)
                                t.reg_mul(r2, ecb, 12)
                                t.reg_add(r1, r1, r2)
                                if isinstance(pb, int):
                                    t.reg_add(r1, r1, 2 * pb + cnt)
                                else:
                                    t.reg_mul(r2, pb, 2)
                                    t.reg_add(r1, r1, r2)
                                    t.reg_add(r1, r1, cnt)
                                t.reg_mul(r1, r1, 16)
                                t.wait_ge(S("ld3"), r1)

                            with FORI(t, 0, 4, step=2) as pb:
                                _ld3wait(tp, ecb, pb, 2)
                                _dicg4(WdA, hsA, 0)
                                _dicg4(WdA, hsA, 1)
                                _ld3wait(tp, ecb, pb, 4)
                                _dicg4(WdB, hsB, 0)
                                _dicg4(WdB, hsB, 1)
                            _ld3wait(tp, ecb, 4, 2)
                            _dicg4(WdA, hsA, 0)
                            _dicg4(WdA, hsA, 1)
                            _ld3wait(tp, ecb, 4, 4)
                            _dicg4(WdB, hsB, 0)
                            for j in range(4):
                                mm = t.matmul(ps_d[j][:], zerot[:], mskA[:, 0:512],
                                              start=False, stop=True)
                            mm.then_inc(S("f2m"), 1)
                        t.reg_mul(r1, tp, 16)
                        t.reg_add(r1, r1, 16)
                        t.wait_ge(S("l3a"), r1)
                        if True:
                            t.reg_mul(r1, tp, 4)
                            t.wait_ge(S("l3d"), r1)
                        t.matmul(ps_s2r[:], zerot[:, 0:1], mskA[:, 0:512],
                                 start=True, stop=False)
                        t.matmul(ps_q2r[:], zerot[:, 0:1], mskA[:, 0:512],
                                 start=True, stop=False)
                        with FORI(t, 0, 16) as ee:
                            t.reg_mul(r1, ee, 512)
                            toff3 = t.snap(r1, min_val=0, max_val=16 * 512 - 512)
                            t.matmul(ps_s2r[:], onesff[:],
                                     AP(x2p, toff3, [[16 * 512, 128], [1, 512]]),
                                     start=False, stop=False)
                            t.reg_mul(r1, tp, 80)
                            t.reg_add(r1, r1, ee)
                            t.reg_add(r1, r1, 1)
                            t.wait_ge(S("l3b"), r1)
                            t.reg_mod(r2, ee, 2)
                            t.reg_mul(r2, r2, 512)
                            toff4 = t.snap(r2, min_val=0, max_val=512)
                            t.matmul(ps_q2r[:], onesff[:],
                                     AP(sq2, toff4, [[1024, 128], [1, 512]]),
                                     start=False, stop=False).then_inc(S("l3c"), 1)
                        t.matmul(ps_s2r[:], zerot[:, 0:1], mskA[:, 0:512],
                                 start=False, stop=True)
                        t.matmul(ps_q2r[:], zerot[:, 0:1], mskA[:, 0:512],
                                 start=False, stop=True).then_inc(S("l3c"), 1)
                        t.reg_mul(r1, tp, 4)
                        t.reg_add(r1, r1, 1)
                        t.wait_ge(S("l3d"), r1)
                        t.reg_mul(r1, tp, 4)
                        t.reg_add(r1, r1, 2)
                        t.wait_ge(S("l3d"), r1)
                        t.matmul(ps_d0[:], onesbf[:], mrow2[:],
                                 start=True, stop=True).then_inc(S("tv2"), 1)
                        t.reg_mul(r1, tp, 4)
                        t.reg_add(r1, r1, 3)
                        t.wait_ge(S("l3d"), r1)
                        t.matmul(ps_d0[:], onesbf[:], rrow2[:],
                                 start=True, stop=True).then_inc(S("tv2"), 1)
                        t.reg_add(r1, tp, 64 + 1)
                        t.wait_ge(S("tv"), r1)
                        for qs in range(4):
                            for ec in range(16):
                                k = 16 * qs + ec
                                t.reg_mul(r2, tp, 80)
                                t.reg_add(r2, r2, 16 + k)
                                t.wait_ge(S("l3b"), r2)
                                t.transpose(ps_t[:],
                                            x2p[:, 512 * ec + 128 * qs:
                                                512 * ec + 128 * (qs + 1)],
                                            identf[:]).then_inc(S("f2m"), 1)

            @blk.scalar
            def _(s):
                with s.register("rg34") as r1, s.register("rg35") as r2:
                    with FORI(s, 0, 4) as tp:
                        with FORI(s, 0, 4) as ecb:
                            s.reg_mul(r1, tp, 48 + 64)
                            s.reg_mul(r2, ecb, 12)
                            s.reg_add(r1, r1, r2)
                            s.reg_add(r1, r1, 12)
                            s.wait_ge(S("f2m"), r1)
                            s.reg_mul(r1, tp, 16)
                            s.reg_mul(r2, ecb, 4)
                            s.reg_add(r1, r1, r2)
                            s.wait_ge(S("l3a"), r1)
                            for j in range(4):
                                s.activation(dbuf[:, 512 * j:512 * (j + 1)],
                                             ps_d[j][:], AF.Copy, bias=0.0,
                                             scale=1.0).then_inc(S("f2e"), 1)
                        s.reg_mul(r1, tp, 16)
                        s.reg_add(r1, r1, 16)
                        s.wait_ge(S("l3a"), r1)
                        with FORI(s, 0, 16) as ee:
                            s.reg_mul(r1, tp, 17)
                            s.reg_add(r1, r1, ee)
                            s.reg_alu(r1, r1, 1, ALU.subtract)
                            s.reg_alu(r1, r1, 0, ALU.max)
                            s.wait_ge(S("l3c"), r1)
                            s.reg_mod(r2, ee, 2)
                            s.reg_mul(r2, r2, 512)
                            soff3 = s.snap(r2, min_val=0, max_val=512)
                            s.reg_mul(r1, ee, 512)
                            soff4 = s.snap(r1, min_val=0, max_val=16 * 512 - 512)
                            s.activation(AP(sq2, soff3, [[1024, 128], [1, 512]]),
                                         AP(x2p, soff4, [[16 * 512, 128], [1, 512]]),
                                         AF.Square, bias=0.0, scale=1.0
                                         ).then_inc(S("l3b"), 1)
                        s.reg_mul(r1, tp, 17)
                        s.reg_add(r1, r1, 17)
                        s.wait_ge(S("l3c"), r1)
                        s.activation(mrow2[:], ps_s2r[:], AF.Copy, bias=0.0,
                                     scale=1.0 / E)
                        s.activation(qrow2[:], ps_q2r[:], AF.Copy, bias=0.0,
                                     scale=1.0 / E).then_inc(S("l3d"), 1)
                        s.reg_add(r1, tp, 4 + 1)
                        s.wait_ge(S("lnC"), r1)
                        s.activation(vrow2[:], vrow2[:], AF.Sqrt, bias=eps1[:],
                                     scale=1.0)
                        s.activation(vrow2[:], vrow2[:], AF.Ln, bias=0.0,
                                     scale=1.0)
                        s.activation(rrow2[:], vrow2[:], AF.Exp, bias=0.0,
                                     scale=-1.0).then_inc(S("l3d"), 1)
                        s.reg_mul(r1, tp, 2)
                        s.reg_add(r1, r1, 1)
                        s.wait_ge(S("tv2"), r1)
                        s.activation(Mb2[:], ps_d0[:], AF.Copy, bias=0.0,
                                     scale=1.0).then_inc(S("l3d"), 1)
                        s.reg_mul(r1, tp, 2)
                        s.reg_add(r1, r1, 2)
                        s.wait_ge(S("tv2"), r1)
                        s.activation(Rb2[:], ps_d0[:], AF.Copy, bias=0.0,
                                     scale=1.0).then_inc(S("l3d"), 1)
                        for qs in range(4):
                            for ec in range(16):
                                k = 16 * qs + ec
                                s.reg_mul(r1, tp, 112)
                                s.reg_add(r1, r1, 48 + k + 1)
                                s.wait_ge(S("f2m"), r1)
                                s.activation(x2tm[:, 128 * ec:128 * (ec + 1)],
                                             ps_t[:], AF.Copy, bias=0.0,
                                             scale=1.0).then_inc(S("l3b"), 1)
                            s.reg_mul(r1, tp, 512)
                            s.reg_add(r1, r1, 128 * qs)
                            s.reg_mul(r1, r1, E)
                            s.dma_start(out=AP(out, r1, [[E, 128], [1, 2048]]),
                                        in_=x2tm[:]).then_inc(S("st"), 16)
                            s.reg_mul(r1, tp, 64)
                            s.reg_add(r1, r1, 16 * qs + 16)
                            s.wait_ge(S("st"), r1)

            @blk.gpsimd
            def _(g):
                with g.register("rg36") as r1, g.register("rg37") as r2:
                    with FORI(g, 0, 4) as tp:
                        g.reg_add(r1, tp, 1)
                        g.reg_mul(r1, r1, 16)
                        g.wait_ge(S("ld3x"), r1)
                        g.reg_mul(r1, tp, 112)
                        g.wait_ge(S("f2m"), r1)
                        for ecb in range(4):
                            for j in range(4):
                                g.reg_mul(r1, tp, 16)
                                g.reg_add(r1, r1, 4 * ecb + j + 1)
                                g.wait_ge(S("f2e"), r1)
                                g.tensor_tensor(
                                    AP(x2p, 512 * (4 * ecb + j),
                                       [[16 * 512, 128], [1, 512]]),
                                    dbuf[:, 512 * j:512 * (j + 1)],
                                    AP(XT1b, 512 * (4 * ecb + j),
                                       [[16 * 512, 128], [1, 512]]),
                                    op=ALU.add).then_inc(S("l3a"), 1)
                        g.reg_mul(r1, tp, 4)
                        g.reg_add(r1, r1, 1)
                        g.wait_ge(S("l3d"), r1)
                        g.tensor_tensor(m2r2[:], mrow2[:], mrow2[:], op=ALU.mult)
                        g.tensor_tensor(vrow2[:], qrow2[:], m2r2[:],
                                        op=ALU.subtract).then_inc(S("lnC"), 1)
                        g.reg_mul(r1, tp, 4)
                        g.reg_add(r1, r1, 4)
                        g.wait_ge(S("l3d"), r1)
                        with FORI(g, 0, 16) as ee:
                            g.reg_mul(r1, ee, 512)
                            goff3 = g.snap(r1, min_val=0, max_val=16 * 512 - 512)
                            g.tensor_tensor(ntf2[:],
                                            AP(x2p, goff3, [[16 * 512, 128], [1, 512]]),
                                            Mb2[:], op=ALU.subtract)
                            g.tensor_tensor(AP(x2p, goff3, [[16 * 512, 128], [1, 512]]),
                                            ntf2[:], Rb2[:], op=ALU.mult)
                        g.sem_inc(S("tv"), 1)

        with nc.Block() as blk:
            @blk.sync
            def _(sy):
                sy.wait_ge(S("st"), 16 * 16)

    return nc



def _host_masks(p):
    k = np.arange(128)[:, None]
    mA = np.zeros((128, 4 * 512), dtype=np.float32)
    mB = np.zeros((128, 4 * 512), dtype=np.float32)
    q = np.arange(512)[None, :]
    for o in range(4):
        mA[:, 512 * o:512 * (o + 1)] = (
            (128 * o + k) <= (512 * (q // 256) + 256 * p + (q % 256)))
        mB[:, 512 * o:512 * (o + 1)] = np.where(
            q < 256, 0.0,
            (128 * (4 + o) + k) <= (256 + 256 * p + q))
    return (np.ascontiguousarray(mA).astype(ml_dtypes.bfloat16),
            np.ascontiguousarray(mB).astype(ml_dtypes.bfloat16))


_NC_CACHE = [None]


def kernel(L0_attn_x, L0_ffn_lut, L0_ffn_gate_idx, L0_ffn_up_idx,
           L0_ffn_down_idx):
    x = np.asarray(L0_attn_x, dtype=np.float32)
    lut = np.asarray(L0_ffn_lut, dtype=np.float32).reshape(1, 16)
    gidx = np.asarray(L0_ffn_gate_idx, dtype=np.int32)
    uidx = np.asarray(L0_ffn_up_idx, dtype=np.int32)
    didx = np.asarray(L0_ffn_down_idx, dtype=np.int32)
    gu = np.ascontiguousarray(np.concatenate([gidx, uidx], axis=0))

    if _NC_CACHE[0] is None:
        _NC_CACHE[0] = build_program()
    nc = _NC_CACHE[0]

    in_maps = []
    for c in range(8):
        b, p = c // 2, c % 2
        xbat = np.ascontiguousarray(x[b])
        rows = np.concatenate(
            [np.arange(256 * (2 * u + p), 256 * (2 * u + p) + 256)
             for u in range(8)])
        xqrows = np.ascontiguousarray(xbat[rows])
        mA, mB = _host_masks(p)
        in_maps.append({"xb": xbat, "xq": xqrows, "lut": lut,
                        "gu_idx": gu, "d_idx": didx, "dmA": mA, "dmB": mB})

    res = run_bass_kernel_spmd(nc, in_maps, core_ids=list(range(8)))
    outp = np.zeros((B, L, E), dtype=np.float32)
    for c in range(8):
        b, p = c // 2, c % 2
        oc = np.asarray(res.results[c]["o"])
        for u in range(8):
            outp[b, 256 * (2 * u + p):256 * (2 * u + p) + 256] = \
                oc[256 * u:256 * (u + 1)]
    return outp
